# version 40
# baseline (speedup 1.0000x reference)
import math
import numpy as np
import concourse.bass as bass
import concourse.mybir as mybir
from concourse.bass_utils import run_bass_kernel_spmd

F32 = mybir.dt.float32
BF16 = mybir.dt.bfloat16
I32 = mybir.dt.int32
AF = mybir.ActivationFunctionType
ALU = mybir.AluOpType

D = 1024
T = 2048
NC = 8
SW = 512
NS = T // SW
NCH = D // 128
NP = NCH // 2
PW = 2 * SW
R = 32

TAU = math.tau
PI = math.pi
C1 = float(np.float32(6.283203125))
C2 = float(np.float32(TAU - C1))
C3 = float(np.float32(TAU - C1 - C2))
C23 = C2 + C3
INV2PI = float(np.float32(1.0 / TAU))


class Plan:

    def __init__(self, nc):
        self.nc = nc
        self.sem = {}
        self.cnt = {}
        for e in "tvsg":
            self.sem[e] = nc.semaphore(f"sem_{e}").__enter__()
            self.cnt[e] = 0
        self.dsem = {}
        self.tags = {}
        self.queue = {e: [] for e in "tvsgd"}
        self.meta = {}
        self.opname = {}

    def op(self, eng, fn, deps=(), tag=None):
        inc_sem = None
        if tag is not None:
            assert tag not in self.tags, tag
            self.cnt[eng] += 1
            self.tags[tag] = ("eng", eng, self.cnt[eng])
            inc_sem = self.sem[eng]
        self.opname[(eng, len(self.queue[eng]))] = tag
        self.queue[eng].append((fn, tuple(deps), inc_sem))

    def dma(self, group, fn, deps=(), tag=None, eng="d"):
        if group not in self.dsem:
            self.dsem[group] = [self.nc.semaphore(f"dma_{group}").__enter__(), 0]
        rec = self.dsem[group]
        rec[1] += 16
        want = rec[1]
        sem = rec[0]

        def f(e):
            return fn(e).then_inc(sem, 16)
        self.meta[(eng, len(self.queue[eng]))] = sem
        self.op(eng, f, deps=deps)
        if tag is not None:
            assert tag not in self.tags, tag
            self.tags[tag] = ("dma", sem, want)

    def alias(self, newtag, oldtag):
        self.tags[newtag] = ("alias", oldtag)

    def resolve(self, tag):
        rec = self.tags[tag]
        while rec[0] == "alias":
            rec = self.tags[rec[1]]
        return rec

    def dma_total(self, group):
        return (self.dsem[group][0], self.dsem[group][1])

    def check(self):
        pos = {e: 0 for e in self.queue}
        done_eng = {e: 0 for e in "tvsg"}
        done_dma = {id(rec[0]): 0 for rec in self.dsem.values()}
        total = sum(len(q) for q in self.queue.values())
        ncomplete = 0
        while ncomplete < total:
            progressed = False
            for e, q in self.queue.items():
                while pos[e] < len(q):
                    fn, deps, inc_sem = q[pos[e]]
                    ok = True
                    for dtag in deps:
                        rec = self.resolve(dtag)
                        if rec[0] == "dma":
                            if done_dma[id(rec[1])] < rec[2]:
                                ok = False
                                break
                        else:
                            if done_eng[rec[1]] < rec[2]:
                                ok = False
                                break
                    if not ok:
                        break
                    meta = self.meta.get((e, pos[e]))
                    if meta is not None:
                        done_dma[id(meta)] = done_dma[id(meta)] + 16
                    if inc_sem is not None:
                        done_eng[e] += 1
                    pos[e] += 1
                    ncomplete += 1
                    progressed = True
            if not progressed:
                print("DEADLOCK: blocked ops:")
                for e, q in self.queue.items():
                    if pos[e] < len(q):
                        fn, deps, inc_sem = q[pos[e]]
                        unmet = []
                        for dtag in deps:
                            try:
                                rec = self.resolve(dtag)
                            except KeyError:
                                unmet.append(dtag + "(unresolved)")
                                continue
                            if rec[0] == "dma":
                                if done_dma[id(rec[1])] < rec[2]:
                                    unmet.append(dtag)
                            elif done_eng[rec[1]] < rec[2]:
                                unmet.append(dtag)
                        name = self.opname.get((e, pos[e]), "?")
                        print(f"  {e}[{pos[e]}] tag={name} unmet={unmet}")
                return False
        print("plan check: no deadlock")
        return True

    def emit(self, block):
        plan = self

        def run(engine_name, eng):
            reached = {}
            for fn, deps, inc_sem in plan.queue[engine_name]:
                for dtag in deps:
                    rec = plan.resolve(dtag)
                    if rec[0] == "dma":
                        sem, val = rec[1], rec[2]
                    else:
                        sem, val = plan.sem[rec[1]], rec[2]
                    if reached.get(id(sem), -1) < val:
                        eng.wait_ge(sem, val)
                        reached[id(sem)] = val
                instr = fn(eng)
                if inc_sem is not None:
                    instr.then_inc(inc_sem, 1)

        @block.tensor
        def _(e):
            run("t", e)

        @block.vector
        def _(e):
            run("v", e)

        @block.scalar
        def _(e):
            run("s", e)

        @block.gpsimd
        def _(e):
            run("g", e)

        @block.sync
        def _(e):
            run("d", e)


def build_program():
    nc = bass.Bass("TRN2", target_bir_lowering=False)

    x_in = nc.dram_tensor("x", [D, T], F32, kind="ExternalInput").ap()
    out = nc.dram_tensor("out", [D, T], F32, kind="ExternalOutput").ap()
    wT = nc.dram_tensor("wT", [D, D], F32, kind="ExternalInput").ap()
    wvT = nc.dram_tensor("wvT", [D, D], F32, kind="ExternalInput").ap()
    woT = nc.dram_tensor("woT", [D, D], F32, kind="ExternalInput").ap()
    vrD = nc.dram_tensor("vr", [D, R], F32, kind="ExternalInput").ap()
    cTD = nc.dram_tensor("cT", [R, D], F32, kind="ExternalInput").ap()
    browD = nc.dram_tensor("brow", [1, D], F32, kind="ExternalInput").ap()
    fcolD = nc.dram_tensor("fcol", [128, NCH], F32, kind="ExternalInput").ap()
    trowD = nc.dram_tensor("trow", [128, T], F32, kind="ExternalInput").ap()

    def sb(name, shape, dt):
        return nc.sbuf_tensor(name, shape, dt).__enter__().ap()

    def psb(name, shape, dt):
        return nc.psum_tensor(name, shape, dt).__enter__().ap()

    Xb = [sb(f"Xb{p}", [128, NCH * SW], F32) for p in range(2)]

    def XP(p_, pc):
        return Xb[p_][:, pc * PW:(pc + 1) * PW]

    arena = sb("arena", [128, 2 * NCH * SW], BF16)

    def AcP(pc):
        return arena[:, pc * PW:(pc + 1) * PW]

    def AsP(pc):
        return arena[:, NCH * SW + pc * PW: NCH * SW + (pc + 1) * PW]

    def Ac1(oc):
        return arena[:, oc * SW:(oc + 1) * SW]

    def As1(oc):
        return arena[:, NCH * SW + oc * SW: NCH * SW + (oc + 1) * SW]

    trow = [sb("trow_s0", [128, SW], F32)] * 2
    tabC = [sb(f"tabC{i}", [128, NCH * SW], BF16) for i in range(2)]
    tabS = [sb(f"tabS{i}", [128, NCH * SW], BF16) for i in range(2)]

    def TCP(s, pc):
        return tabC[s % 2][:, pc * PW:(pc + 1) * PW]

    def TSP(s, pc):
        return tabS[s % 2][:, pc * PW:(pc + 1) * PW]

    b16a = sb("b16a", [128, NCH * SW], BF16)
    b16d = sb("b16d", [128, NCH * SW], BF16)
    b16b = sb("b16b", [128, NCH * SW], BF16)
    bsq = sb("bsq", [128, NCH * SW], BF16)

    def B16(buf, dc):
        return buf[:, dc * SW:(dc + 1) * SW]

    w16 = sb("w16", [128, NCH * D], BF16)
    wv16 = sb("wv16", [128, NCH * D], BF16)
    wo16 = sb("wo16", [128, NCH * D], BF16)

    def Wsl(w, dc, oc):
        return w[:, dc * D + oc * 128: dc * D + (oc + 1) * 128]

    vraug = sb("vraug", [128, NCH * (R + 1)], BF16)

    def VR(dc):
        return vraug[:, dc * (R + 1):(dc + 1) * (R + 1)]

    cTb = sb("cTb", [R, D], BF16)
    z16 = [sb(f"z16_{i}", [R, SW], BF16) for i in range(2)]
    ones16c = sb("ones16c", [128, 1], BF16)
    onesf32r = sb("onesf32r", [1, 128], F32)
    halff32r = sb("halff32r", [1, 128], F32)
    mhcol = sb("mhcol", [128, 1], F32)
    phcol = sb("phcol", [128, 1], F32)
    epscol = sb("epscol", [128, 1], F32)
    c1col = sb("c1col", [128, 1], F32)
    c23col = sb("c23col", [128, 1], F32)
    brow16 = sb("brow16", [1, D], BF16)
    rows32 = sb("rows32", [128, SW], F32)
    rows32b = sb("rows32b", [128, SW], F32)
    rows32c = sb("rows32c", [128, SW], F32)
    svar = rows32b[96:97, :]
    lnrow = rows32c[96:97, :]
    rrow = sb("rrow", [1, SW], F32)
    sd16 = sb("sd16", [1, SW], BF16)
    stot16 = sb("stot16", [1, D], BF16)
    swv16 = sb("swv16", [1, D], BF16)
    svr16 = sb("svr16", [R, 1], BF16)
    mnegs = [sb(f"mneg{i}", [1, SW], BF16) for i in range(3)]
    Rrep = [sb(f"Rrep{i}", [128, SW], F32) for i in range(2)]
    ccar = sb("ccar", [128, 2 * NCH], BF16)
    fcs = sb("fcs", [128, NCH], F32)
    finv = sb("finv", [128, NCH], F32)
    phs = [sb("phs0", [128, PW], F32)] * 2
    uus = [sb("uus0", [128, PW], F32)] * 2
    qis = [sb("qis0", [128, PW], I32)] * 2
    scrE = sb("scrE", [128, PW], F32)

    psP = [psb(f"psP{i}", [128, PW], F32) for i in range(2)]
    psZB = psb("psZB", [128, SW], F32)
    psZ = psZB[0:R + 1, :]
    psQ1 = psb("psQ1", [1, SW], F32)
    psQ2 = psb("psQ2", [1, SW], F32)
    psR = psb("psR", [128, SW], F32)

    p = Plan(nc)
    bank_state = {"k": 0, "free": [None, None]}

    def next_bank(free_tag):
        idx = bank_state["k"] % 2
        bank_state["k"] += 1
        dep = bank_state["free"][idx]
        bank_state["free"][idx] = free_tag
        return psP[idx], dep

    def rep2(ap):
        return ap.rearrange("p (o t) -> p o t", o=1).broadcast_to([128, 2, SW])

    def SR(ln):
        return rows32[32 * ln:32 * ln + 1, :]

    def SRQ(ln):
        return rows32b[32 * ln:32 * ln + 1, :]

    def SVT(ln):
        return rows32c[32 * ln:32 * ln + 1, :]

    def MN(ln):
        return mnegs[ln]

    def lhs_stot(oc):
        return stot16[:, oc * 128:(oc + 1) * 128]

    def lhs_swv(oc):
        return swv16[:, oc * 128:(oc + 1) * 128]

    def lhs_b(oc):
        return brow16[:, oc * 128:(oc + 1) * 128]

    def wload(dram, dest16, group):
        for dc in range(NCH):
            p.dma(group, lambda e, d=dram, dc=dc, dest16=dest16: e.dma_start(
                out=dest16[:, dc * D:(dc + 1) * D],
                in_=d[dc * 128:(dc + 1) * 128, :]),
                tag=f"{group}16" if dc == NCH - 1 else None, eng="g")

    for dc in range(NCH):
        p.dma("vrg", lambda e, dc=dc: e.dma_start(
            out=VR(dc)[:, 0:R], in_=vrD[dc * 128:(dc + 1) * 128, :]), eng="g",
            tag="vr_dma" if dc == NCH - 1 else None)
    p.dma("smallg", lambda e: e.dma_start(out=cTb, in_=cTD), eng="g")
    p.dma("smallg", lambda e: e.dma_start(out=brow16, in_=browD),
          tag="smallg_dma", eng="g")
    wload(wT, w16, "w")
    wload(wvT, wv16, "wv")
    wload(woT, wo16, "wo")
    p.dma("small", lambda e: e.dma_start(out=fcs, in_=fcolD), tag="small_dma")
    p.dma("trow0", lambda e: e.dma_start(out=trow[0], in_=trowD[:, 0:SW]),
          tag="trow_dma0")

    p.op("v", lambda e: e.memset(ones16c, 1.0))
    p.op("v", lambda e: e.memset(onesf32r, 1.0))
    p.op("v", lambda e: e.memset(halff32r, 0.5))
    p.op("v", lambda e: e.memset(mhcol, -0.5))
    p.op("v", lambda e: e.memset(phcol, 0.5))
    p.op("v", lambda e: e.memset(epscol, 1e-5))
    p.op("v", lambda e: e.memset(c1col, C1))
    p.op("v", lambda e: e.memset(c23col, C23))
    p.op("v", lambda e: e.memset(ccar, 0.0), tag="memsets")
    p.op("v", lambda e: e.tensor_scalar(out=finv, in0=fcs, scalar1=INV2PI,
                                        scalar2=None, op0=ALU.mult),
         deps=("small_dma",), tag="finv")
    for dc in range(NCH):
        p.op("v", lambda e, dc=dc: e.memset(VR(dc)[:, R:R + 1], 1.0),
             deps=("vr_dma",) if dc == 0 else (),
             tag="vraug" if dc == NCH - 1 else None)


    def load_x(s, deps):
        for oc in range(NCH):
            p.dma(f"x{s % 2}", lambda e, s=s, oc=oc: e.dma_start(
                out=Xb[s % 2][:, oc * SW:(oc + 1) * SW],
                in_=x_in[oc * 128:(oc + 1) * 128, s * SW:(s + 1) * SW]),
                deps=deps if oc == 0 else (),
                tag=f"xin{s}" if oc == NCH - 1 else None)

    load_x(0, ())
    load_x(1, ())

    tab_pending = []

    def drip(n=1):
        for _ in range(n):
            if not tab_pending:
                return
            eng, fn, deps, tag = tab_pending.pop(0)
            p.op(eng, fn, deps=deps, tag=tag)

    def drip_all():
        drip(len(tab_pending))

    def gen_tables(u, pcs=None, defer=False):
        def q(eng, fn, deps=(), tag=None):
            if defer:
                tab_pending.append((eng, fn, deps, tag))
            else:
                p.op(eng, fn, deps=deps, tag=tag)
        tdeps = [f"trow_dma{u}", "small_dma", "finv", "memsets"]
        if u >= 2:
            tdeps.append(f"recomb{u-2}_done")
        if u == 0:
            tdeps.append("vraug")
        for pc in (range(NP) if pcs is None else pcs):
            ph = phs[pc % 2]
            uu = uus[pc % 2]
            qi = qis[pc % 2]
            sdeps = list(tdeps) if pc == 0 else [f"tab{u}_sin{pc-1}"]
            if pc == 0 and u > 0:
                sdeps.append(f"tab{u-1}_sin{NP-1}")
            sdeps = tuple(sdeps)
            for h in range(2):
                oc = 2 * pc + h
                q("s", lambda e, u=u, oc=oc, h=h, ph=ph: e.activation(
                    out=ph[:, h * SW:(h + 1) * SW], in_=trow[u % 2], func=AF.Identity,
                    scale=fcs[:, oc:oc + 1]),
                    deps=sdeps if h == 0 else (),
                    tag=f"tab{u}_ph{oc}" if h == 1 else None)
                q("s", lambda e, u=u, oc=oc, h=h, qi=qi: e.activation(
                    out=qi[:, h * SW:(h + 1) * SW], in_=trow[u % 2], func=AF.Identity,
                    scale=finv[:, oc:oc + 1]),
                    tag=f"tab{u}_qi{oc}" if h == 1 else None)
            q("s", lambda e, uu=uu, qi=qi: e.activation(
                out=uu, in_=qi, func=AF.Identity, scale=c1col),
                deps=(f"tab{u}_qi{2*pc+1}",), tag=f"tab{u}_qc1_{pc}")
            q("v", lambda e, ph=ph, uu=uu: e.tensor_tensor(
                out=ph, in0=ph, in1=uu, op=ALU.subtract),
                deps=(f"tab{u}_qc1_{pc}", f"tab{u}_ph{2*pc+1}"), tag=f"tab{u}_r1_{pc}")
            q("s", lambda e, uu=uu, qi=qi: e.activation(
                out=uu, in_=qi, func=AF.Identity, scale=c23col),
                deps=(f"tab{u}_r1_{pc}",), tag=f"tab{u}_qc23_{pc}")
            q("v", lambda e, ph=ph, uu=uu: e.tensor_tensor(
                out=ph, in0=ph, in1=uu, op=ALU.subtract),
                deps=(f"tab{u}_qc23_{pc}",), tag=f"tab{u}_r2_{pc}")
            q("v", lambda e, ph=ph, uu=uu: e.tensor_scalar(
                out=uu, in0=ph, scalar1=PI, scalar2=TAU, op0=ALU.is_gt, op1=ALU.mult),
                deps=(f"tab{u}_r2_{pc}",), tag=f"tab{u}_mt_{pc}")
            q("v", lambda e, ph=ph, uu=uu: e.tensor_tensor(
                out=ph, in0=ph, in1=uu, op=ALU.subtract),
                deps=(f"tab{u}_mt_{pc}",), tag=f"tab{u}_wr_{pc}")
            q("v", lambda e, ph=ph: e.tensor_scalar(
                out=ph, in0=ph, scalar1=PI, scalar2=-PI, op0=ALU.min, op1=ALU.max),
                deps=(f"tab{u}_wr_{pc}",), tag=f"tab{u}_r{pc}")
            q("s", lambda e, u=u, pc=pc, ph=ph: e.activation(
                out=TSP(u, pc), in_=ph, func=AF.Sin),
                deps=(f"tab{u}_r{pc}",), tag=f"tab{u}_ts{pc}")
            q("s", lambda e, ph=ph, uu=uu: e.activation(out=uu, in_=ph, func=AF.Abs),
                 deps=(f"tab{u}_r{pc}",), tag=f"tab{u}_abs{pc}")
            q("v", lambda e, ph=ph, uu=uu: e.tensor_scalar(
                out=ph, in0=uu, scalar1=-1.0, scalar2=PI / 2, op0=ALU.mult, op1=ALU.add),
                deps=(f"tab{u}_abs{pc}", f"tab{u}_ts{pc}"), tag=f"tab{u}_arg{pc}")
            q("s", lambda e, u=u, pc=pc, ph=ph: e.activation(
                out=TCP(u, pc), in_=ph, func=AF.Sin),
                deps=(f"tab{u}_arg{pc}",), tag=f"tab{u}_sin{pc}")
        if (pcs is None) or (NP - 1 in pcs):
            p.alias(f"tab{u}_done", f"tab{u}_sin{NP-1}")


    for dc in range(NCH):
        p.op("t", lambda e, dc=dc: e.matmul(out=psZ[:, 0:1], lhsT=VR(dc), rhs=ones16c,
                                            start=(dc == 0), stop=(dc == NCH - 1)),
             deps=("vraug", "memsets") if dc == 0 else (),
             tag="svr_ps" if dc == NCH - 1 else None)
    p.op("v", lambda e: e.tensor_copy(out=svr16, in_=psZ[0:R, 0:1]),
         deps=("svr_ps",), tag="svr16")
    stot_bank, _ = next_bank("stot16_1")
    for half in range(2):
        dst = stot_bank[0:1, half * SW:(half + 1) * SW]
        for dc in range(NCH):
            p.op("t", lambda e, dc=dc, dst=dst, half=half: e.matmul(
                out=dst, lhsT=ones16c,
                rhs=w16[:, dc * D + half * SW: dc * D + half * SW + SW],
                start=(dc == 0), stop=False),
                deps=("w16", "svr16", "smallg_dma") if dc == 0 else ())
        p.op("t", lambda e, dst=dst, half=half: e.matmul(
            out=dst, lhsT=svr16, rhs=cTb[:, half * SW:half * SW + SW],
            start=False, stop=True), tag=f"stot_ps{half}")
    swv_bank, _ = next_bank("swv16_1")
    for half in range(2):
        dst = swv_bank[0:1, half * SW:(half + 1) * SW]
        for dc in range(NCH):
            p.op("t", lambda e, dc=dc, dst=dst, half=half: e.matmul(
                out=dst, lhsT=ones16c,
                rhs=wv16[:, dc * D + half * SW: dc * D + half * SW + SW],
                start=(dc == 0), stop=(dc == NCH - 1)),
                deps=("wv16", "stot16_1") if dc == 0 else (),
                tag=f"swv_ps{half}" if dc == NCH - 1 else None)

    gen_tables(0, pcs=[0, 1])

    for half in range(2):
        dst = stot_bank[0:1, half * SW:(half + 1) * SW]
        p.op("s", lambda e, dst=dst, half=half: e.copy(
            out=stot16[:, half * SW:half * SW + SW], in_=dst),
            deps=(f"stot_ps{half}",), tag=f"stot16_{half}")
    for half in range(2):
        dst = swv_bank[0:1, half * SW:(half + 1) * SW]
        p.op("s", lambda e, dst=dst, half=half: e.copy(
            out=swv16[:, half * SW:half * SW + SW], in_=dst),
            deps=(f"swv_ps{half}",), tag=f"swv16_{half}")
    p.alias("stats_banks_free", "swv16_1")

    def emit_xsq1(u):
        sqdeps = [f"x16_{u}"]
        if u > 0:
            sqdeps.append(f"ffn2{u-1}_q2ps")
        p.op("s", lambda e: e.activation(out=bsq, in_=b16d, func=AF.Square),
             deps=tuple(sqdeps), tag=f"xsq1_{u}")

    def emit_x16_casts(u):
        for pc in range(NP):
            cdeps = [f"xin{u}"]
            if pc == 0 and u > 0:
                cdeps += [f"ffn1{u-1}_lastmm", f"ffn1{u-1}_zps", f"xsq1_{u-1}",
                          f"ln2{u-1}_q1ps"]
            p.op("s", lambda e, u=u, pc=pc: e.activation(
                out=b16d[:, pc * PW:(pc + 1) * PW], in_=Xb[u % 2][:, pc * PW:(pc + 1) * PW],
                func=AF.Identity),
                deps=tuple(cdeps), tag=f"x16p{u}_{pc}")
        p.alias(f"x16_{u}", f"x16p{u}_{NP-1}")

    for s in range(NS):
        par = s % 2

        if s == 0:
            emit_x16_casts(0)
            emit_xsq1(0)
            gen_tables(0, pcs=[2, 3])
            p.dma("trowg1", lambda e: e.dma_start(
                out=trow[1], in_=trowD[:, SW:2 * SW]),
                deps=("tab0_qi7",), tag="trow_dma1", eng="g")
        if s == 0:
            zdeps = ["vraug", "stats_banks_free"]
            for dc in range(NCH):
                dd = list(zdeps) if dc == 0 else []
                if dc % 2 == 0:
                    dd.append(f"x16p{s}_{dc//2}")
                p.op("t", lambda e, dc=dc: e.matmul(
                    out=psZ, lhsT=VR(dc), rhs=B16(b16d, dc),
                    start=(dc == 0), stop=(dc == NCH - 1)),
                    deps=tuple(dd),
                    tag=f"ffn1{s}_zps" if dc == NCH - 1 else None)
        qb1, qb1dep = next_bank(f"ln1_{s}_srowq_done")
        psQ2a = qb1[0:1, 0:SW]
        qdeps = [f"xsq1_{s}"]
        if qb1dep is not None:
            qdeps.append(qb1dep)
        for dc in range(NCH):
            p.op("t", lambda e, dc=dc, psQ2a=psQ2a: e.matmul(
                out=psQ2a, lhsT=ones16c, rhs=B16(bsq, dc),
                start=(dc == 0), stop=(dc == NCH - 1)),
                 deps=tuple(qdeps) if dc == 0 else (),
                 tag=f"ffn1{s}_q2ps" if dc == NCH - 1 else None)
        if s == 0:
            p.op("v", lambda e: e.tensor_copy(out=z16[0], in_=psZ[0:R, :]),
                 deps=(f"ffn1{s}_zps",), tag=f"z16ready{s}_0")
            p.op("s", lambda e: e.copy(out=SR(0), in_=psZ[R:R + 1, :]),
                 deps=(f"ffn1{s}_zps",), tag=f"ln1_{s}_srow_done")
        p.op("s", lambda e, psQ2a=psQ2a: e.copy(out=SRQ(0), in_=psQ2a),
             deps=(f"ffn1{s}_q2ps",)
                  + ((f"ln1_{s-1}_var",) if s > 0 else ()),
             tag=f"ln1_{s}_srowq_done")

        mnd = [f"ln1_{s}_srow_done"]
        if s > 0:
            mnd.append(f"ffn1{s-1}_lastmm")
        p.op("v", lambda e: e.tensor_scalar(
            out=MN(0), in0=SR(0), scalar1=-1.0 / D, scalar2=None, op0=ALU.mult),
            deps=tuple(mnd), tag=f"ffn1{s}_mneg")
        m2d = [f"ln1_{s}_srow_done"]
        if s > 0:
            m2d.append(f"ln1_{s-1}_var")
        p.op("s", lambda e: e.activation(out=SVT(0), in_=SR(0), func=AF.Square,
                                         scale=1.0 / D),
             deps=tuple(m2d), tag=f"ln1_{s}_m2")
        p.op("v", lambda e: e.scalar_tensor_tensor(
            out=svar, in0=SRQ(0), scalar=1.0 / D, in1=SVT(0),
            op0=ALU.mult, op1=ALU.subtract),
            deps=(f"ln1_{s}_srowq_done", f"ln1_{s}_m2")
                 + ((f"ln3_{s-1}_ln",) if s > 0 else ()), tag=f"ln1_{s}_var")
        lndeps = [f"ln1_{s}_var"]
        if s > 0:
            lndeps += [f"ln3_{s-1}_psR", f"ln3_{s-1}_sd", f"ln3_{s-1}_rr"]
        p.op("s", lambda e: e.activation(out=lnrow, in_=svar, func=AF.Ln,
                                         bias=epscol[0:1, :]),
             deps=tuple(lndeps), tag=f"ln1_{s}_ln")
        p.op("s", lambda e: e.activation(out=rrow, in_=lnrow, func=AF.Exp,
                                         scale=mhcol[0:1, :]),
             deps=(f"ln1_{s}_ln",) + ((f"ln3_{s-1}_psR",) if s > 0 else ()),
             tag=f"ln1_{s}_rr")
        sdd = [f"ln1_{s}_ln"]
        if s > 0:
            sdd.append(f"ffn2{s-1}_bK1")
        p.op("s", lambda e: e.activation(out=sd16, in_=lnrow, func=AF.Exp,
                                         scale=phcol[0:1, :]),
             deps=tuple(sdd), tag=f"ln1_{s}_sd")
        def emit_psR1():
            rdeps = [f"ln1_{s}_rr", "stats_banks_free"]
            if s > 0:
                rdeps.append(f"ln3_{s-1}_R")
            p.op("t", lambda e: e.matmul(out=psR, lhsT=halff32r, rhs=rrow,
                                         start=True, stop=True),
                 deps=tuple(rdeps), tag=f"ln1_{s}_psR")
            rcdeps = [f"ln1_{s}_psR"]
            if s > 0:
                rcdeps.append(f"ffn2{s-1}_Rused")
            p.op("s", lambda e: e.copy(out=Rrep[0], in_=psR),
                 deps=tuple(rcdeps), tag=f"ln1_{s}_R")

        def proj_ffn(name, zbuf_idx, mneg_ln, rrep_idx, rsrc_tag, xdep_fn,
                     mid_emit=None, xsrc=None):
            banks = {}

            def mains(pc):
                bank, bdep = next_bank(f"{name}_t{pc}")
                banks[pc] = bank
                mmdeps = ["w16"]
                if bdep is not None:
                    mmdeps.append(bdep)
                for h in range(2):
                    oc = 2 * pc + h
                    dst = bank[:, h * SW:(h + 1) * SW]
                    for dc in range(NCH):
                        dd = list(mmdeps) if (h == 0 and dc == 0) else []
                        if dc % 2 == 0:
                            dd.append(xdep_fn(dc // 2))
                        p.op("t", lambda e, dc=dc, oc=oc, dst=dst, xsrc=xsrc: e.matmul(
                            out=dst, lhsT=Wsl(w16, dc, oc), rhs=B16(xsrc, dc),
                            start=(dc == 0), stop=False),
                            deps=tuple(dd))

            def tail(pc):
                bank = banks[pc]
                for h in range(2):
                    oc = 2 * pc + h
                    dst = bank[:, h * SW:(h + 1) * SW]
                    p.op("t", lambda e, oc=oc, dst=dst, zbuf_idx=zbuf_idx: e.matmul(
                        out=dst, lhsT=cTb[:, oc * 128:(oc + 1) * 128], rhs=z16[zbuf_idx],
                        start=False, stop=False),
                        deps=(f"{name}_zdep", "smallg_dma") if (pc == 0 and h == 0) else (),
                        tag=f"{name}_z2done" if oc == NCH - 1 else None)
                    p.op("t", lambda e, oc=oc, dst=dst, mneg_ln=mneg_ln: e.matmul(
                        out=dst, lhsT=lhs_stot(oc), rhs=MN(mneg_ln),
                        start=False, stop=False),
                        deps=(f"{name}_mnegdep",) if (pc == 0 and h == 0) else ())
                    p.op("t", lambda e, oc=oc, dst=dst: e.matmul(
                        out=dst, lhsT=lhs_b(oc), rhs=sd16,
                        start=False, stop=True),
                        deps=(f"{name}_sddep", "smallg_dma") if (pc == 0 and h == 0) else (),
                        tag=f"{name}_mm{pc}" if h == 1 else None)

            evq = []

            def evac(pc):
                bank = banks[pc]
                edeps = [f"{name}_mm{pc}", rsrc_tag]
                if pc > 0:
                    edeps.append(f"{name}_ev{pc-1}")
                else:
                    edeps.append(f"{name}_scrdep")
                p.op("v", lambda e, bank=bank, rrep_idx=rrep_idx: e.tensor_tensor(
                    out=scrE, in0=bank, in1=rep2(Rrep[rrep_idx]), op=ALU.mult),
                    deps=tuple(edeps), tag=f"{name}_t{pc}")
                p.op("v", lambda e, par=par, pc=pc: e.tensor_tensor(
                    out=XP(par, pc), in0=scrE, in1=XP(par, pc), op=ALU.add),
                    deps=(f"{name}_t{pc}",), tag=f"{name}_ev{pc}")
                drip(2)

            mains(0)
            mains(1)
            if mid_emit is not None:
                mid_emit()
            tail(0)
            evac(0)
            mains(2)
            tail(1)
            evac(1)
            mains(3)
            tail(2)
            evac(2)
            tail(3)
            evac(3)
            p.alias(f"{name}_bK1", f"{name}_mm{NP-1}")
            p.alias(f"{name}_Rused", f"{name}_t{NP-1}")
            p.alias(f"{name}_lastmm", f"{name}_mm{NP-1}")

        if s + 1 < NS:
            if s + 1 >= 2:
                p.dma(f"trowg{s + 1}", lambda e, s=s: e.dma_start(
                    out=trow[(s + 1) % 2], in_=trowD[:, (s + 1) * SW:(s + 2) * SW]),
                    deps=(f"tab{s}_qi{NCH-1}",), tag=f"trow_dma{s+1}", eng="g")
            if s >= 1:
                gen_tables(s + 1, defer=True)

        nm = f"ffn1{s}"
        if s > 0:
            p.alias(f"{nm}_bankdep0", f"ffn2{s-1}_ev{NP-2}")
            p.alias(f"{nm}_bankdep1", f"ffn2{s-1}_ev{NP-1}")
            p.alias(f"{nm}_scrdep", f"ffn2{s-1}_ev{NP-1}")
        else:
            p.alias(f"{nm}_bankdep0", "stats_banks_free")
            p.alias(f"{nm}_bankdep1", "stats_banks_free")
            p.alias(f"{nm}_scrdep", "memsets")
        p.alias(f"{nm}_zdep", f"z16ready{s}_0")
        p.alias(f"{nm}_mnegdep", f"{nm}_mneg")
        p.alias(f"{nm}_sddep", f"ln1_{s}_sd")
        proj_ffn(nm, 0, 0, 0, f"ln1_{s}_R", lambda pc2, s=s: f"x16p{s}_{pc2}",
                 mid_emit=emit_psR1, xsrc=b16d)

        for pc in range(NP):
            c1deps = [f"ffn1{s}_ev{pc}"]
            if pc == 0 and s > 0:
                c1deps.append(f"wo{s-1}_lastmm")
            p.op("s", lambda e, par=par, pc=pc: e.activation(
                out=b16b[:, pc * PW:(pc + 1) * PW], in_=XP(par, pc),
                func=AF.Identity),
                deps=tuple(c1deps), tag=f"x116p{s}_{pc}")
        p.alias(f"x116_{s}", f"x116p{s}_{NP-1}")

        p.op("s", lambda e: e.activation(out=bsq, in_=b16b, func=AF.Square),
             deps=(f"x116_{s}", f"ffn1{s}_q2ps"), tag=f"xsq2_{s}")
        qb2, qb2dep = next_bank(f"ln2_{s}_srowq_done")
        psQ1b = qb2[0:1, 0:SW]
        psQ2b = qb2[0:1, SW:PW]
        q1deps = [f"x116_{s}"]
        if qb2dep is not None:
            q1deps.append(qb2dep)
        for dc in range(NCH):
            p.op("t", lambda e, dc=dc, psQ1b=psQ1b: e.matmul(
                out=psQ1b, lhsT=ones16c, rhs=B16(b16b, dc),
                start=(dc == 0), stop=(dc == NCH - 1)),
                 deps=tuple(q1deps) if dc == 0 else (),
                 tag=f"ln2{s}_q1ps" if dc == NCH - 1 else None)
        def emit_q2_ln2():
            for dc in range(NCH):
                p.op("t", lambda e, dc=dc, psQ2b=psQ2b: e.matmul(
                    out=psQ2b, lhsT=ones16c, rhs=B16(bsq, dc),
                    start=(dc == 0), stop=(dc == NCH - 1)),
                    deps=(f"xsq2_{s}",) if dc == 0 else (),
                    tag=f"ln2{s}_q2ps" if dc == NCH - 1 else None)
        p.op("s", lambda e, psQ1b=psQ1b: e.copy(out=SR(1), in_=psQ1b),
             deps=(f"ln2{s}_q1ps",)
                  + ((f"v{s-1}_mm{NP-1}", f"ln2_{s-1}_var", f"ln2_{s-1}_m2") if s > 0 else ()),
             tag=f"ln2_{s}_srow_done")
        p.op("s", lambda e, psQ2b=psQ2b: e.copy(out=SRQ(1), in_=psQ2b),
             deps=(f"ln2{s}_q2ps",) + ((f"ln2_{s-1}_var",) if s > 0 else ()),
             tag=f"ln2_{s}_srowq_done")
        mnd2 = [f"ln2_{s}_srow_done"]
        if s > 0:
            mnd2.append(f"v{s-1}_mm{NP-1}")
        p.op("v", lambda e: e.tensor_scalar(
            out=MN(1), in0=SR(1), scalar1=-1.0 / D, scalar2=None, op0=ALU.mult),
            deps=tuple(mnd2), tag=f"v{s}_mneg")
        p.op("s", lambda e: e.activation(out=SVT(1), in_=SR(1), func=AF.Square,
                                         scale=1.0 / D),
             deps=(f"ln2_{s}_srow_done", f"ln1_{s}_rr", f"ln1_{s}_sd")
                  + ((f"ln2_{s-1}_var",) if s > 0 else ()), tag=f"ln2_{s}_m2")
        p.op("v", lambda e: e.scalar_tensor_tensor(
            out=svar, in0=SRQ(1), scalar=1.0 / D, in1=SVT(1),
            op0=ALU.mult, op1=ALU.subtract),
            deps=(f"ln2_{s}_srowq_done", f"ln2_{s}_m2", f"ln1_{s}_ln"),
            tag=f"ln2_{s}_var")
        p.op("s", lambda e: e.activation(out=lnrow, in_=svar, func=AF.Ln,
                                         bias=epscol[0:1, :]),
             deps=(f"ln2_{s}_var", f"ln1_{s}_psR", f"ln1_{s}_sd"), tag=f"ln2_{s}_ln")
        p.op("s", lambda e: e.activation(out=rrow, in_=lnrow, func=AF.Exp,
                                         scale=mhcol[0:1, :]),
             deps=(f"ln2_{s}_ln",), tag=f"ln2_{s}_rr")
        def emit_psR2():
            p.op("t", lambda e: e.matmul(out=psR, lhsT=onesf32r, rhs=rrow,
                                         start=True, stop=True),
                 deps=(f"ln2_{s}_rr", f"ln1_{s}_R"), tag=f"ln2_{s}_psR")
        rc2 = [f"ln2_{s}_psR"]
        if s > 0:
            rc2.append(f"v{s-1}_Rused")
        p.op("s", lambda e: e.copy(out=Rrep[1], in_=psR),
             deps=tuple(rc2), tag=f"ln2_{s}_R")
        if s + 1 < NS:
            emit_x16_casts(s + 1)


        if s == 0:
            drip_all()
            gen_tables(1, defer=True)

        def emit_recombine(pc):
            t1deps = [f"scanc{s}_{2*pc}", f"scanc{s}_{2*pc+1}", f"ln2{s}_q2ps"]
            if pc > 0:
                t1deps.append(f"ret{s}_{pc-1}")
            p.op("v", lambda e, s=s, pc=pc: e.tensor_tensor(
                out=bsq[:, pc * PW:(pc + 1) * PW], in0=AcP(pc), in1=TCP(s, pc),
                op=ALU.mult),
                deps=tuple(t1deps), tag=f"t1_{s}_{pc}")
            t2deps = [f"scans{s}_{2*pc}", f"scans{s}_{2*pc+1}",
                      f"car{s}_s{2*pc}", f"car{s}_s{2*pc+1}"]
            p.op("v", lambda e, s=s, pc=pc: e.tensor_tensor(
                out=AsP(pc), in0=AsP(pc), in1=TSP(s, pc), op=ALU.mult),
                deps=tuple(t2deps), tag=f"t2_{s}_{pc}")
            p.op("v", lambda e, s=s, pc=pc: e.tensor_tensor(
                out=bsq[:, pc * PW:(pc + 1) * PW],
                in0=bsq[:, pc * PW:(pc + 1) * PW], in1=AsP(pc), op=ALU.add),
                deps=(f"t1_{s}_{pc}", f"t2_{s}_{pc}"), tag=f"ret{s}_{pc}")
            drip(2)

        for pc in range(NP):
            if pc == 0:
                emit_q2_ln2()
            if pc == 1:
                emit_psR2()
            bank, bdep = next_bank(f"v{s}_vv{pc}")
            mmdeps = ["wv16"]
            if bdep is not None:
                mmdeps.append(bdep)
            for h in range(2):
                oc = 2 * pc + h
                dst = bank[:, h * SW:(h + 1) * SW]
                for dc in range(NCH):
                    dd = list(mmdeps) if (h == 0 and dc == 0) else []
                    if dc % 2 == 0:
                        dd.append(f"x116p{s}_{dc//2}")
                    p.op("t", lambda e, dc=dc, oc=oc, dst=dst: e.matmul(
                        out=dst, lhsT=Wsl(wv16, dc, oc), rhs=B16(b16b, dc),
                        start=(dc == 0), stop=False),
                        deps=tuple(dd))
                p.op("t", lambda e, oc=oc, dst=dst: e.matmul(
                    out=dst, lhsT=lhs_swv(oc), rhs=MN(1),
                    start=False, stop=True),
                    deps=(f"v{s}_mneg",) if (pc == 0 and h == 0) else (),
                    tag=f"v{s}_mm{pc}" if h == 1 else None)
            mdeps = [f"v{s}_mm{pc}", f"ln2_{s}_R", f"tab{s}_done"]
            if s > 0 and pc < 2:
                mdeps.append(f"recomb{s-1}_done")
            elif s > 0 and pc == 2:
                mdeps.append(f"scan{s-1}_done")
            p.op("v", lambda e, s=s, pc=pc, bank=bank: e.tensor_tensor(
                out=AcP(pc), in0=bank, in1=rep2(Rrep[1]), op=ALU.mult),
                deps=tuple(mdeps), tag=f"v{s}_vv{pc}")
            p.op("v", lambda e, s=s, pc=pc: e.tensor_tensor(
                out=AsP(pc), in0=AcP(pc), in1=TSP(s, pc), op=ALU.mult),
                deps=(f"v{s}_vv{pc}",), tag=f"v{s}_as{pc}")
            p.op("v", lambda e, s=s, pc=pc: e.tensor_tensor(
                out=AcP(pc), in0=AcP(pc), in1=TCP(s, pc), op=ALU.mult),
                deps=(f"v{s}_as{pc}",), tag=f"v{s}_ac{pc}")
            drip(2)
            if pc > 0:
                emit_recombine(pc - 1)
            for oc in (2 * pc, 2 * pc + 1):
                if s == 0:
                    p.op("v", lambda e, oc=oc: e.tensor_tensor_scan(
                        out=Ac1(oc), data0=Ac1(oc), data1=Ac1(oc), initial=0.0,
                        op0=ALU.add, op1=ALU.bypass),
                        deps=(f"v{s}_ac{pc}",), tag=f"scanc{s}_{oc}")
                    p.op("v", lambda e, oc=oc: e.tensor_tensor_scan(
                        out=As1(oc), data0=As1(oc), data1=As1(oc), initial=0.0,
                        op0=ALU.add, op1=ALU.bypass),
                        deps=(f"v{s}_as{pc}",), tag=f"scans{s}_{oc}")
                else:
                    p.op("v", lambda e, oc=oc: e.tensor_tensor_scan(
                        out=Ac1(oc), data0=Ac1(oc), data1=Ac1(oc),
                        initial=ccar[:, oc:oc + 1], op0=ALU.add, op1=ALU.bypass),
                        deps=(f"v{s}_ac{pc}", f"car{s-1}_c{oc}"), tag=f"scanc{s}_{oc}")
                    p.op("v", lambda e, oc=oc: e.tensor_tensor_scan(
                        out=As1(oc), data0=As1(oc), data1=As1(oc),
                        initial=ccar[:, NCH + oc:NCH + oc + 1], op0=ALU.add,
                        op1=ALU.bypass),
                        deps=(f"v{s}_as{pc}", f"car{s-1}_s{oc}"), tag=f"scans{s}_{oc}")
                p.op("s", lambda e, oc=oc: e.copy(
                    out=ccar[:, oc:oc + 1], in_=Ac1(oc)[:, SW - 1:SW]),
                    deps=(f"scanc{s}_{oc}",), tag=f"car{s}_c{oc}")
                p.op("s", lambda e, oc=oc: e.copy(
                    out=ccar[:, NCH + oc:NCH + oc + 1], in_=As1(oc)[:, SW - 1:SW]),
                    deps=(f"scans{s}_{oc}",), tag=f"car{s}_s{oc}")
                drip(2)
        p.alias(f"v{s}_Rused", f"v{s}_vv{NP-1}")
        p.alias(f"scan{s}_done", f"car{s}_s{NCH-1}")
        emit_recombine(NP - 1)
        p.alias(f"recomb{s}_done", f"ret{s}_{NP-1}")

        if s + 1 < NS:
            u = s + 1
            zdeps = [f"ffn1{s}_z2done"]
            if s == 0:
                zdeps += [f"z16ready{0}_0", f"ln1_{0}_srow_done"]
            else:
                zdeps += [f"z16ready{s-1}_1", f"ln3_{s-1}_srow_done"]
            for dc in range(NCH):
                dd = list(zdeps) if dc == 0 else []
                if dc % 2 == 0:
                    dd.append(f"x16p{u}_{dc//2}")
                p.op("t", lambda e, dc=dc: e.matmul(
                    out=psZ, lhsT=VR(dc), rhs=B16(b16d, dc),
                    start=(dc == 0), stop=(dc == NCH - 1)),
                    deps=tuple(dd),
                    tag=f"ffn1{u}_zps" if dc == NCH - 1 else None)
            p.op("s", lambda e: e.activation(out=z16[0], in_=psZ[0:R, :],
                                             func=AF.Identity),
                 deps=(f"ffn1{u}_zps",), tag=f"z16ready{u}_0")
            p.op("s", lambda e: e.copy(out=SR(0), in_=psZ[R:R + 1, :]),
                 deps=(f"ffn1{u}_zps", f"ln1_{s}_var", f"ln1_{s}_m2"),
                 tag=f"ln1_{u}_srow_done")

        for pc in range(NP - 1):
            bank, bdep = next_bank(f"wo{s}_ev{pc}")
            mmdeps = [f"wo16"]
            if bdep is not None:
                mmdeps.append(bdep)
            for h in range(2):
                oc = 2 * pc + h
                dst = bank[:, h * SW:(h + 1) * SW]
                for dc in range(NCH):
                    dd = list(mmdeps) if (h == 0 and dc == 0) else []
                    if dc % 2 == 0:
                        dd.append(f"ret{s}_{dc//2}")
                    p.op("t", lambda e, dc=dc, oc=oc, dst=dst: e.matmul(
                        out=dst, lhsT=Wsl(wo16, dc, oc), rhs=B16(bsq, dc),
                        start=(dc == 0), stop=(dc == NCH - 1)),
                        deps=tuple(dd),
                        tag=f"wo{s}_mm{pc}" if (h == 1 and dc == NCH - 1) else None)
            evdeps = [f"wo{s}_mm{pc}"]
            if pc > 0:
                evdeps.append(f"wo{s}_ev{pc-1}")
            else:
                evdeps.append(f"ffn1{s}_ev{NP-1}")
            p.op("v", lambda e, par=par, pc=pc, bank=bank: e.tensor_tensor(
                out=XP(par, pc), in0=bank, in1=XP(par, pc), op=ALU.add),
                deps=tuple(evdeps), tag=f"wo{s}_ev{pc}")
            drip(2)
        if s + 1 < NS:
            pz_free = [f"z16ready{s+1}_0", f"ln1_{s+1}_srow_done"]
        else:
            pz_free = [f"z16ready{s-1}_1", f"ln3_{s-1}_srow_done"]
        for h, (dst, extra) in enumerate([(psZB, pz_free),
                                          (psR, [f"ln2_{s}_R"])]):
            oc = 2 * (NP - 1) + h
            for dc in range(NCH):
                dd = (["wo16"] + extra) if dc == 0 else []
                if dc % 2 == 0:
                    dd.append(f"ret{s}_{dc//2}")
                p.op("t", lambda e, dc=dc, oc=oc, dst=dst: e.matmul(
                    out=dst, lhsT=Wsl(wo16, dc, oc), rhs=B16(bsq, dc),
                    start=(dc == 0), stop=(dc == NCH - 1)),
                    deps=tuple(dd),
                    tag=f"wo{s}_mm3{'ab'[h]}" if dc == NCH - 1 else None)
        p.alias(f"wo{s}_mm{NP-1}", f"wo{s}_mm3b")
        p.op("v", lambda e, par=par: e.tensor_tensor(
            out=XP(par, NP - 1)[:, 0:SW], in0=psZB, in1=XP(par, NP - 1)[:, 0:SW],
            op=ALU.add),
            deps=(f"wo{s}_mm3a", f"wo{s}_ev{NP-2}"), tag=f"wo{s}_ev3a")
        p.op("v", lambda e, par=par: e.tensor_tensor(
            out=XP(par, NP - 1)[:, SW:PW], in0=psR, in1=XP(par, NP - 1)[:, SW:PW],
            op=ALU.add),
            deps=(f"wo{s}_mm3b", f"wo{s}_ev3a"), tag=f"wo{s}_ev{NP-1}")
        drip(2)
        p.alias(f"wo{s}_lastmm", f"wo{s}_mm{NP-1}")

        for pc in range(NP):
            c2deps = [f"wo{s}_ev{pc}"]
            if pc == 0:
                c2deps += [f"ffn1{s}_lastmm", f"ffn1{s}_zps", f"xsq1_{s}"]
            p.op("s", lambda e, par=par, pc=pc: e.activation(
                out=b16a[:, pc * PW:(pc + 1) * PW], in_=XP(par, pc),
                func=AF.Identity),
                deps=tuple(c2deps),
                tag=f"x216p{s}_{pc}" if True else None)
        p.alias(f"x216_{s}", f"x216p{s}_{NP-1}")
        p.op("s", lambda e: e.activation(out=bsq, in_=b16a, func=AF.Square),
             deps=(f"x216_{s}", f"ln2{s}_q2ps"), tag=f"xsq3_{s}")
        z2deps = [f"z16ready{s}_0", f"ln1_{s}_srow_done", f"wo{s}_ev3a"]
        if s + 1 < NS:
            z2deps += [f"z16ready{s+1}_0", f"ln1_{s+1}_srow_done"]
        if s > 0:
            z2deps.append(f"ffn2{s-1}_z2done")
        for dc in range(NCH):
            dd = list(z2deps) if dc == 0 else []
            if dc % 2 == 0:
                dd.append(f"x216p{s}_{dc//2}")
            p.op("t", lambda e, dc=dc: e.matmul(out=psZ, lhsT=VR(dc), rhs=B16(b16a, dc),
                                                start=(dc == 0), stop=(dc == NCH - 1)),
                 deps=tuple(dd),
                 tag=f"ffn2{s}_zps" if dc == NCH - 1 else None)
        qb3, qb3dep = next_bank(f"ln3_{s}_srowq_done")
        psQ2c = qb3[0:1, 0:SW]
        q3deps = [f"xsq3_{s}"]
        if qb3dep is not None:
            q3deps.append(qb3dep)
        for dc in range(NCH):
            p.op("t", lambda e, dc=dc, psQ2c=psQ2c: e.matmul(
                out=psQ2c, lhsT=ones16c, rhs=B16(bsq, dc),
                start=(dc == 0), stop=(dc == NCH - 1)),
                 deps=tuple(q3deps) if dc == 0 else (),
                 tag=f"ffn2{s}_q2ps" if dc == NCH - 1 else None)
        zev2 = [f"ffn2{s}_zps"]
        if s > 0:
            zev2.append(f"ffn2{s-1}_z2done")
        p.op("v", lambda e: e.tensor_copy(out=z16[1], in_=psZ[0:R, :]),
             deps=tuple(zev2), tag=f"z16ready{s}_1")
        p.op("s", lambda e: e.copy(out=SR(2), in_=psZ[R:R + 1, :]),
             deps=(f"ffn2{s}_zps", f"ln2_{s}_var")
                  + ((f"ffn2{s-1}_lastmm", f"ln3_{s-1}_var", f"ln3_{s-1}_m2") if s > 0 else ()),
             tag=f"ln3_{s}_srow_done")
        p.op("s", lambda e, psQ2c=psQ2c: e.copy(out=SRQ(2), in_=psQ2c),
             deps=(f"ffn2{s}_q2ps",) + ((f"ln3_{s-1}_var",) if s > 0 else ()),
             tag=f"ln3_{s}_srowq_done")
        mnd3 = [f"ln3_{s}_srow_done"]
        if s > 0:
            mnd3.append(f"ffn2{s-1}_lastmm")
        p.op("v", lambda e: e.tensor_scalar(
            out=MN(2), in0=SR(2), scalar1=-1.0 / D, scalar2=None, op0=ALU.mult),
            deps=tuple(mnd3), tag=f"ffn2{s}_mneg")
        p.op("s", lambda e: e.activation(out=SVT(2), in_=SR(2), func=AF.Square,
                                         scale=1.0 / D),
             deps=(f"ln3_{s}_srow_done", f"ln2_{s}_rr")
                  + ((f"ln3_{s-1}_var",) if s > 0 else ()), tag=f"ln3_{s}_m2")
        p.op("v", lambda e: e.scalar_tensor_tensor(
            out=svar, in0=SRQ(2), scalar=1.0 / D, in1=SVT(2),
            op0=ALU.mult, op1=ALU.subtract),
            deps=(f"ln3_{s}_srowq_done", f"ln3_{s}_m2", f"ln2_{s}_ln"),
            tag=f"ln3_{s}_var")
        p.op("s", lambda e: e.activation(out=lnrow, in_=svar, func=AF.Ln,
                                         bias=epscol[0:1, :]),
             deps=(f"ln3_{s}_var", f"ln2_{s}_psR", f"ln1_{s}_sd"), tag=f"ln3_{s}_ln")
        p.op("s", lambda e: e.activation(out=rrow, in_=lnrow, func=AF.Exp,
                                         scale=mhcol[0:1, :]),
             deps=(f"ln3_{s}_ln",), tag=f"ln3_{s}_rr")
        p.op("s", lambda e: e.activation(out=sd16, in_=lnrow, func=AF.Exp,
                                         scale=phcol[0:1, :]),
             deps=(f"ln3_{s}_ln", f"ffn1{s}_bK1"), tag=f"ln3_{s}_sd")
        def emit_psR3():
            p.op("t", lambda e: e.matmul(out=psR, lhsT=halff32r, rhs=rrow,
                                         start=True, stop=True),
                 deps=(f"ln3_{s}_rr", f"ln2_{s}_R", f"wo{s}_ev{NP-1}"),
                 tag=f"ln3_{s}_psR")
        p.op("s", lambda e: e.copy(out=Rrep[0], in_=psR),
             deps=(f"ln3_{s}_psR", f"ffn1{s}_Rused"), tag=f"ln3_{s}_R")
        if s + 1 < NS:
            emit_xsq1(s + 1)

        nm = f"ffn2{s}"
        p.alias(f"{nm}_bankdep0", f"wo{s}_ev{NP-2}")
        p.alias(f"{nm}_bankdep1", f"wo{s}_ev{NP-1}")
        p.alias(f"{nm}_scrdep", f"ffn1{s}_ev{NP-1}")
        p.alias(f"{nm}_zdep", f"z16ready{s}_1")
        p.alias(f"{nm}_mnegdep", f"{nm}_mneg")
        p.alias(f"{nm}_sddep", f"ln3_{s}_sd")
        proj_ffn(nm, 1, 2, 0, f"ln3_{s}_R", lambda pc2, s=s: f"x216p{s}_{pc2}",
                 mid_emit=emit_psR3, xsrc=b16a)

        drip_all()
        for oc in range(NCH):
            pc = oc // 2
            p.dma(f"o{par}", lambda e, s=s, par=par, oc=oc: e.dma_start(
                out=out[oc * 128:(oc + 1) * 128, s * SW:(s + 1) * SW],
                in_=Xb[par][:, oc * SW:(oc + 1) * SW]),
                deps=(f"ffn2{s}_ev{pc}",) if oc % 2 == 0 else (),
                tag=f"outdma{s}" if oc == NCH - 1 else None)
        if s + 2 < NS:
            load_x(s + 2, (f"outdma{s}",))

    s0, v0 = p.dma_total("o0")
    s1, v1 = p.dma_total("o1")

    def final_wait(e):
        e.wait_ge(s0, v0)
        e.wait_ge(s1, v1)
        return None
    p.op("d", final_wait)

    assert p.check(), "plan deadlock"
    with nc.Block() as block:
        p.emit(block)
    return nc


_NC_CACHE = None


def kernel(L0_attn_x, L0_attn_Wk, L0_attn_Wv, L0_attn_Wo, L0_attn_freqs,
           L0_ffn_W, L0_ffn_b, L0_ffn_V_r, L0_ffn_C):
    global _NC_CACHE
    x = np.asarray(L0_attn_x, dtype=np.float32)
    B = x.shape[0]
    wTh = np.ascontiguousarray(np.asarray(L0_ffn_W, np.float32).T)
    wvTh = np.ascontiguousarray(np.asarray(L0_attn_Wv, np.float32).T)
    woTh = np.ascontiguousarray(np.asarray(L0_attn_Wo, np.float32).T)
    vrh = np.ascontiguousarray(np.asarray(L0_ffn_V_r, np.float32))
    cTh = np.ascontiguousarray(np.asarray(L0_ffn_C, np.float32).T)
    browh = np.ascontiguousarray(np.asarray(L0_ffn_b, np.float32).reshape(1, D))
    fcolh = np.ascontiguousarray(np.asarray(L0_attn_freqs, np.float32).reshape(NCH, 128).T)
    trowh = np.ascontiguousarray(np.tile(np.arange(T, dtype=np.float32), (128, 1)))

    if _NC_CACHE is None:
        _NC_CACHE = build_program()
    nc = _NC_CACHE

    in_maps = []
    for c in range(B):
        in_maps.append({
            "x": np.ascontiguousarray(x[c].T),
            "wT": wTh, "wvT": wvTh, "woT": woTh, "vr": vrh, "cT": cTh,
            "brow": browh, "fcol": fcolh, "trow": trowh,
        })
    res = run_bass_kernel_spmd(nc, in_maps, core_ids=list(range(NC)))
    outs = [np.ascontiguousarray(np.asarray(r["out"]).T) for r in res.results]
    return np.stack(outs, axis=0).astype(np.float32)


# version 41 (final)
# speedup vs baseline: 1.1831x; 1.1831x over previous
import math
import numpy as np
import concourse.bass as bass
import concourse.mybir as mybir
from concourse.bass_utils import run_bass_kernel_spmd

F32 = mybir.dt.float32
BF16 = mybir.dt.bfloat16
I32 = mybir.dt.int32
AF = mybir.ActivationFunctionType
ALU = mybir.AluOpType

D = 1024
T = 2048
NC = 8
SW = 512
NS = T // SW
NCH = D // 128
NP = NCH // 2
PW = 2 * SW
R = 32

TAU = math.tau
PI = math.pi
C1 = float(np.float32(6.283203125))
C2 = float(np.float32(TAU - C1))
C3 = float(np.float32(TAU - C1 - C2))
C23 = C2 + C3
INV2PI = float(np.float32(1.0 / TAU))


class Plan:

    def __init__(self, nc):
        self.nc = nc
        self.sem = {}
        self.cnt = {}
        for e in "tvsg":
            self.sem[e] = nc.semaphore(f"sem_{e}").__enter__()
            self.cnt[e] = 0
        self.dsem = {}
        self.tags = {}
        self.queue = {e: [] for e in "tvsgd"}
        self.meta = {}
        self.opname = {}

    def op(self, eng, fn, deps=(), tag=None):
        inc_sem = None
        if tag is not None:
            assert tag not in self.tags, tag
            self.cnt[eng] += 1
            self.tags[tag] = ("eng", eng, self.cnt[eng])
            inc_sem = self.sem[eng]
        self.opname[(eng, len(self.queue[eng]))] = tag
        self.queue[eng].append((fn, tuple(deps), inc_sem))

    def dma(self, group, fn, deps=(), tag=None, eng="d"):
        if group not in self.dsem:
            self.dsem[group] = [self.nc.semaphore(f"dma_{group}").__enter__(), 0]
        rec = self.dsem[group]
        rec[1] += 16
        want = rec[1]
        sem = rec[0]

        def f(e):
            return fn(e).then_inc(sem, 16)
        self.meta[(eng, len(self.queue[eng]))] = sem
        self.op(eng, f, deps=deps)
        if tag is not None:
            assert tag not in self.tags, tag
            self.tags[tag] = ("dma", sem, want)

    def alias(self, newtag, oldtag):
        self.tags[newtag] = ("alias", oldtag)

    def resolve(self, tag):
        rec = self.tags[tag]
        while rec[0] == "alias":
            rec = self.tags[rec[1]]
        return rec

    def dma_total(self, group):
        return (self.dsem[group][0], self.dsem[group][1])

    def check(self):
        pos = {e: 0 for e in self.queue}
        done_eng = {e: 0 for e in "tvsg"}
        done_dma = {id(rec[0]): 0 for rec in self.dsem.values()}
        total = sum(len(q) for q in self.queue.values())
        ncomplete = 0
        while ncomplete < total:
            progressed = False
            for e, q in self.queue.items():
                while pos[e] < len(q):
                    fn, deps, inc_sem = q[pos[e]]
                    ok = True
                    for dtag in deps:
                        rec = self.resolve(dtag)
                        if rec[0] == "dma":
                            if done_dma[id(rec[1])] < rec[2]:
                                ok = False
                                break
                        else:
                            if done_eng[rec[1]] < rec[2]:
                                ok = False
                                break
                    if not ok:
                        break
                    meta = self.meta.get((e, pos[e]))
                    if meta is not None:
                        done_dma[id(meta)] = done_dma[id(meta)] + 16
                    if inc_sem is not None:
                        done_eng[e] += 1
                    pos[e] += 1
                    ncomplete += 1
                    progressed = True
            if not progressed:
                print("DEADLOCK: blocked ops:")
                for e, q in self.queue.items():
                    if pos[e] < len(q):
                        fn, deps, inc_sem = q[pos[e]]
                        unmet = []
                        for dtag in deps:
                            try:
                                rec = self.resolve(dtag)
                            except KeyError:
                                unmet.append(dtag + "(unresolved)")
                                continue
                            if rec[0] == "dma":
                                if done_dma[id(rec[1])] < rec[2]:
                                    unmet.append(dtag)
                            elif done_eng[rec[1]] < rec[2]:
                                unmet.append(dtag)
                        name = self.opname.get((e, pos[e]), "?")
                        print(f"  {e}[{pos[e]}] tag={name} unmet={unmet}")
                return False
        print("plan check: no deadlock")
        return True

    def emit(self, block):
        plan = self

        def run(engine_name, eng):
            reached = {}
            for fn, deps, inc_sem in plan.queue[engine_name]:
                for dtag in deps:
                    rec = plan.resolve(dtag)
                    if rec[0] == "dma":
                        sem, val = rec[1], rec[2]
                    else:
                        sem, val = plan.sem[rec[1]], rec[2]
                    if reached.get(id(sem), -1) < val:
                        eng.wait_ge(sem, val)
                        reached[id(sem)] = val
                instr = fn(eng)
                if inc_sem is not None:
                    instr.then_inc(inc_sem, 1)

        @block.tensor
        def _(e):
            run("t", e)

        @block.vector
        def _(e):
            run("v", e)

        @block.scalar
        def _(e):
            run("s", e)

        @block.gpsimd
        def _(e):
            run("g", e)

        @block.sync
        def _(e):
            run("d", e)


def build_program():
    nc = bass.Bass("TRN2", target_bir_lowering=False)

    x_in = nc.dram_tensor("x", [D, T], F32, kind="ExternalInput").ap()
    out = nc.dram_tensor("out", [D, T], F32, kind="ExternalOutput").ap()
    wT = nc.dram_tensor("wT", [D, D], F32, kind="ExternalInput").ap()
    wvT = nc.dram_tensor("wvT", [D, D], F32, kind="ExternalInput").ap()
    woT = nc.dram_tensor("woT", [D, D], F32, kind="ExternalInput").ap()
    vrD = nc.dram_tensor("vr", [D, R], F32, kind="ExternalInput").ap()
    cTD = nc.dram_tensor("cT", [R, D], F32, kind="ExternalInput").ap()
    browD = nc.dram_tensor("brow", [1, D], F32, kind="ExternalInput").ap()
    fcolD = nc.dram_tensor("fcol", [128, NCH], F32, kind="ExternalInput").ap()
    trowD = nc.dram_tensor("trow", [128, T], F32, kind="ExternalInput").ap()

    def sb(name, shape, dt):
        return nc.sbuf_tensor(name, shape, dt).__enter__().ap()

    def psb(name, shape, dt):
        return nc.psum_tensor(name, shape, dt).__enter__().ap()

    Xb = [sb(f"Xb{p}", [128, NCH * SW], F32) for p in range(2)]

    def XP(p_, pc):
        return Xb[p_][:, pc * PW:(pc + 1) * PW]

    arena = sb("arena", [128, 2 * NCH * SW], BF16)

    def AcP(pc):
        return arena[:, pc * PW:(pc + 1) * PW]

    def AsP(pc):
        return arena[:, NCH * SW + pc * PW: NCH * SW + (pc + 1) * PW]

    def Ac1(oc):
        return arena[:, oc * SW:(oc + 1) * SW]

    def As1(oc):
        return arena[:, NCH * SW + oc * SW: NCH * SW + (oc + 1) * SW]

    trow = [sb("trow_s0", [128, SW], F32)] * 2
    tabC = [sb(f"tabC{i}", [128, NCH * SW], BF16) for i in range(2)]
    tabS = [sb(f"tabS{i}", [128, NCH * SW], BF16) for i in range(2)]

    def TCP(s, pc):
        return tabC[s % 2][:, pc * PW:(pc + 1) * PW]

    def TSP(s, pc):
        return tabS[s % 2][:, pc * PW:(pc + 1) * PW]

    b16a = sb("b16a", [128, NCH * SW], BF16)
    b16d = sb("b16d", [128, NCH * SW], BF16)
    b16b = sb("b16b", [128, NCH * SW], BF16)
    bsq = sb("bsq", [128, NCH * SW], BF16)

    def B16(buf, dc):
        return buf[:, dc * SW:(dc + 1) * SW]

    w16 = sb("w16", [128, NCH * D], BF16)
    wv16 = sb("wv16", [128, NCH * D], BF16)
    wo16 = sb("wo16", [128, NCH * D], BF16)

    def Wsl(w, dc, oc):
        return w[:, dc * D + oc * 128: dc * D + (oc + 1) * 128]

    vraug = sb("vraug", [128, NCH * (R + 1)], BF16)

    def VR(dc):
        return vraug[:, dc * (R + 1):(dc + 1) * (R + 1)]

    cTb = sb("cTb", [R, D], BF16)
    z16 = [sb(f"z16_{i}", [R, SW], BF16) for i in range(2)]
    ones16c = sb("ones16c", [128, 1], BF16)
    onesf32r = sb("onesf32r", [1, 128], F32)
    halff32r = sb("halff32r", [1, 128], F32)
    mhcol = sb("mhcol", [128, 1], F32)
    phcol = sb("phcol", [128, 1], F32)
    epscol = sb("epscol", [128, 1], F32)
    c1col = sb("c1col", [128, 1], F32)
    c23col = sb("c23col", [128, 1], F32)
    brow16 = sb("brow16", [1, D], BF16)
    rows32 = sb("rows32", [128, SW], F32)
    rows32b = sb("rows32b", [128, SW], F32)
    rows32c = sb("rows32c", [128, SW], F32)
    svar = rows32b[96:97, :]
    lnrow = rows32c[96:97, :]
    rrow = sb("rrow", [1, SW], F32)
    sd16 = sb("sd16", [1, SW], BF16)
    stot16 = sb("stot16", [1, D], BF16)
    swv16 = sb("swv16", [1, D], BF16)
    svr16 = sb("svr16", [R, 1], BF16)
    mnegs = [sb(f"mneg{i}", [1, SW], BF16) for i in range(3)]
    Rrep = [sb(f"Rrep{i}", [128, SW], F32) for i in range(2)]
    ccar = sb("ccar", [128, 2 * NCH], BF16)
    fcs = sb("fcs", [128, NCH], F32)
    finv = sb("finv", [128, NCH], F32)
    phs = [sb("phs0", [128, PW], F32)] * 2
    uus = [sb("uus0", [128, PW], F32)] * 2
    qis = [sb("qis0", [128, PW], I32)] * 2
    scrE = sb("scrE", [128, PW], F32)

    psP = [psb(f"psP{i}", [128, PW], F32) for i in range(2)]
    psZB = psb("psZB", [128, SW], F32)
    psZ = psZB[0:R + 1, :]
    psQ1 = psb("psQ1", [1, SW], F32)
    psQ2 = psb("psQ2", [1, SW], F32)
    psR = psb("psR", [128, SW], F32)

    p = Plan(nc)
    bank_state = {"k": 0, "free": [None, None]}

    def next_bank(free_tag):
        idx = bank_state["k"] % 2
        bank_state["k"] += 1
        dep = bank_state["free"][idx]
        bank_state["free"][idx] = free_tag
        return psP[idx], dep

    def rep2(ap):
        return ap.rearrange("p (o t) -> p o t", o=1).broadcast_to([128, 2, SW])

    def SR(ln):
        return rows32[32 * ln:32 * ln + 1, :]

    def SRQ(ln):
        return rows32b[32 * ln:32 * ln + 1, :]

    def SVT(ln):
        return rows32c[32 * ln:32 * ln + 1, :]

    def MN(ln):
        return mnegs[ln]

    def lhs_stot(oc):
        return stot16[:, oc * 128:(oc + 1) * 128]

    def lhs_swv(oc):
        return swv16[:, oc * 128:(oc + 1) * 128]

    def lhs_b(oc):
        return brow16[:, oc * 128:(oc + 1) * 128]

    def wload(dram, dest16, group):
        for dc in range(NCH):
            p.dma(group, lambda e, d=dram, dc=dc, dest16=dest16: e.dma_start(
                out=dest16[:, dc * D:(dc + 1) * D],
                in_=d[dc * 128:(dc + 1) * 128, :]),
                tag=f"{group}16" if dc == NCH - 1 else None, eng="g")

    for dc in range(NCH):
        p.dma("vrg", lambda e, dc=dc: e.dma_start(
            out=VR(dc)[:, 0:R], in_=vrD[dc * 128:(dc + 1) * 128, :]), eng="g",
            tag="vr_dma" if dc == NCH - 1 else None)
    p.dma("smallg", lambda e: e.dma_start(out=cTb, in_=cTD), eng="g")
    p.dma("smallg", lambda e: e.dma_start(out=brow16, in_=browD),
          tag="smallg_dma", eng="g")
    wload(wT, w16, "w")
    wload(wvT, wv16, "wv")
    wload(woT, wo16, "wo")
    p.dma("small", lambda e: e.dma_start(out=fcs, in_=fcolD), tag="small_dma")
    p.dma("trow0", lambda e: e.dma_start(out=trow[0], in_=trowD[:, 0:SW]),
          tag="trow_dma0")

    p.op("v", lambda e: e.memset(ones16c, 1.0))
    p.op("v", lambda e: e.memset(onesf32r, 1.0))
    p.op("v", lambda e: e.memset(halff32r, 0.5))
    p.op("v", lambda e: e.memset(mhcol, -0.5))
    p.op("v", lambda e: e.memset(phcol, 0.5))
    p.op("v", lambda e: e.memset(epscol, 1e-5))
    p.op("v", lambda e: e.memset(c1col, C1))
    p.op("v", lambda e: e.memset(c23col, C23))
    p.op("v", lambda e: e.memset(ccar, 0.0), tag="memsets")
    p.op("v", lambda e: e.tensor_scalar(out=finv, in0=fcs, scalar1=INV2PI,
                                        scalar2=None, op0=ALU.mult),
         deps=("small_dma",), tag="finv")
    for dc in range(NCH):
        p.op("v", lambda e, dc=dc: e.memset(VR(dc)[:, R:R + 1], 1.0),
             deps=("vr_dma",) if dc == 0 else (),
             tag="vraug" if dc == NCH - 1 else None)


    def load_x(s, deps):
        for oc in range(NCH):
            p.dma(f"x{s % 2}", lambda e, s=s, oc=oc: e.dma_start(
                out=Xb[s % 2][:, oc * SW:(oc + 1) * SW],
                in_=x_in[oc * 128:(oc + 1) * 128, s * SW:(s + 1) * SW]),
                deps=deps if oc == 0 else (),
                tag=f"xin{s}" if oc == NCH - 1 else None)

    load_x(0, ())
    load_x(1, ())

    tab_pending = []

    def drip(n=1):
        for _ in range(n):
            if not tab_pending:
                return
            eng, fn, deps, tag = tab_pending.pop(0)
            p.op(eng, fn, deps=deps, tag=tag)

    def drip_all():
        drip(len(tab_pending))

    def gen_tables(u, pcs=None, defer=False):
        def q(eng, fn, deps=(), tag=None):
            if defer:
                tab_pending.append((eng, fn, deps, tag))
            else:
                p.op(eng, fn, deps=deps, tag=tag)
        tdeps = [f"trow_dma{u}", "small_dma", "finv", "memsets"]
        if u >= 2:
            tdeps.append(f"recomb{u-2}_done")
        if u == 0:
            tdeps.append("vraug")
        for pc in (range(NP) if pcs is None else pcs):
            ph = phs[pc % 2]
            uu = uus[pc % 2]
            qi = qis[pc % 2]
            sdeps = list(tdeps) if pc == 0 else [f"tab{u}_sin{pc-1}"]
            if pc == 0 and u > 0:
                sdeps.append(f"tab{u-1}_sin{NP-1}")
            sdeps = tuple(sdeps)
            for h in range(2):
                oc = 2 * pc + h
                q("s", lambda e, u=u, oc=oc, h=h, ph=ph: e.activation(
                    out=ph[:, h * SW:(h + 1) * SW], in_=trow[u % 2], func=AF.Identity,
                    scale=fcs[:, oc:oc + 1]),
                    deps=sdeps if h == 0 else (),
                    tag=f"tab{u}_ph{oc}" if h == 1 else None)
                q("s", lambda e, u=u, oc=oc, h=h, qi=qi: e.activation(
                    out=qi[:, h * SW:(h + 1) * SW], in_=trow[u % 2], func=AF.Identity,
                    scale=finv[:, oc:oc + 1]),
                    tag=f"tab{u}_qi{oc}" if h == 1 else None)
            q("s", lambda e, uu=uu, qi=qi: e.activation(
                out=uu, in_=qi, func=AF.Identity, scale=c1col),
                deps=(f"tab{u}_qi{2*pc+1}",), tag=f"tab{u}_qc1_{pc}")
            q("v", lambda e, ph=ph, uu=uu: e.tensor_tensor(
                out=ph, in0=ph, in1=uu, op=ALU.subtract),
                deps=(f"tab{u}_qc1_{pc}", f"tab{u}_ph{2*pc+1}"), tag=f"tab{u}_r1_{pc}")
            q("s", lambda e, uu=uu, qi=qi: e.activation(
                out=uu, in_=qi, func=AF.Identity, scale=c23col),
                deps=(f"tab{u}_r1_{pc}",), tag=f"tab{u}_qc23_{pc}")
            q("v", lambda e, ph=ph, uu=uu: e.tensor_tensor(
                out=ph, in0=ph, in1=uu, op=ALU.subtract),
                deps=(f"tab{u}_qc23_{pc}",), tag=f"tab{u}_r2_{pc}")
            q("v", lambda e, ph=ph, uu=uu: e.tensor_scalar(
                out=uu, in0=ph, scalar1=PI, scalar2=TAU, op0=ALU.is_gt, op1=ALU.mult),
                deps=(f"tab{u}_r2_{pc}",), tag=f"tab{u}_mt_{pc}")
            q("v", lambda e, ph=ph, uu=uu: e.tensor_tensor(
                out=ph, in0=ph, in1=uu, op=ALU.subtract),
                deps=(f"tab{u}_mt_{pc}",), tag=f"tab{u}_wr_{pc}")
            q("v", lambda e, ph=ph: e.tensor_scalar(
                out=ph, in0=ph, scalar1=PI, scalar2=-PI, op0=ALU.min, op1=ALU.max),
                deps=(f"tab{u}_wr_{pc}",), tag=f"tab{u}_r{pc}")
            q("s", lambda e, u=u, pc=pc, ph=ph: e.activation(
                out=TSP(u, pc), in_=ph, func=AF.Sin),
                deps=(f"tab{u}_r{pc}",), tag=f"tab{u}_ts{pc}")
            q("s", lambda e, ph=ph, uu=uu: e.activation(out=uu, in_=ph, func=AF.Abs),
                 deps=(f"tab{u}_r{pc}",), tag=f"tab{u}_abs{pc}")
            q("v", lambda e, ph=ph, uu=uu: e.tensor_scalar(
                out=ph, in0=uu, scalar1=-1.0, scalar2=PI / 2, op0=ALU.mult, op1=ALU.add),
                deps=(f"tab{u}_abs{pc}", f"tab{u}_ts{pc}"), tag=f"tab{u}_arg{pc}")
            q("s", lambda e, u=u, pc=pc, ph=ph: e.activation(
                out=TCP(u, pc), in_=ph, func=AF.Sin),
                deps=(f"tab{u}_arg{pc}",), tag=f"tab{u}_sin{pc}")
        if (pcs is None) or (NP - 1 in pcs):
            p.alias(f"tab{u}_done", f"tab{u}_sin{NP-1}")


    for dc in range(NCH):
        p.op("t", lambda e, dc=dc: e.matmul(out=psZ[:, 0:1], lhsT=VR(dc), rhs=ones16c,
                                            start=(dc == 0), stop=(dc == NCH - 1)),
             deps=("vraug", "memsets") if dc == 0 else (),
             tag="svr_ps" if dc == NCH - 1 else None)
    p.op("v", lambda e: e.tensor_copy(out=svr16, in_=psZ[0:R, 0:1]),
         deps=("svr_ps",), tag="svr16")
    stot_bank, _ = next_bank("stot16_1")
    for half in range(2):
        dst = stot_bank[0:1, half * SW:(half + 1) * SW]
        for dc in range(NCH):
            p.op("t", lambda e, dc=dc, dst=dst, half=half: e.matmul(
                out=dst, lhsT=ones16c,
                rhs=w16[:, dc * D + half * SW: dc * D + half * SW + SW],
                start=(dc == 0), stop=False),
                deps=("w16", "svr16", "smallg_dma") if dc == 0 else ())
        p.op("t", lambda e, dst=dst, half=half: e.matmul(
            out=dst, lhsT=svr16, rhs=cTb[:, half * SW:half * SW + SW],
            start=False, stop=True), tag=f"stot_ps{half}")
    swv_bank, _ = next_bank("swv16_1")
    for half in range(2):
        dst = swv_bank[0:1, half * SW:(half + 1) * SW]
        for dc in range(NCH):
            p.op("t", lambda e, dc=dc, dst=dst, half=half: e.matmul(
                out=dst, lhsT=ones16c,
                rhs=wv16[:, dc * D + half * SW: dc * D + half * SW + SW],
                start=(dc == 0), stop=(dc == NCH - 1)),
                deps=("wv16", "stot16_1") if dc == 0 else (),
                tag=f"swv_ps{half}" if dc == NCH - 1 else None)

    gen_tables(0, pcs=[0, 1])

    for half in range(2):
        dst = stot_bank[0:1, half * SW:(half + 1) * SW]
        p.op("s", lambda e, dst=dst, half=half: e.copy(
            out=stot16[:, half * SW:half * SW + SW], in_=dst),
            deps=(f"stot_ps{half}",), tag=f"stot16_{half}")
    for half in range(2):
        dst = swv_bank[0:1, half * SW:(half + 1) * SW]
        p.op("s", lambda e, dst=dst, half=half: e.copy(
            out=swv16[:, half * SW:half * SW + SW], in_=dst),
            deps=(f"swv_ps{half}",), tag=f"swv16_{half}")
    p.alias("stats_banks_free", "swv16_1")

    def emit_xsq1(u):
        sqdeps = [f"x16_{u}"]
        if u > 0:
            sqdeps.append(f"ffn2{u-1}_q2ps")
        p.op("s", lambda e: e.activation(out=bsq, in_=b16d, func=AF.Square),
             deps=tuple(sqdeps), tag=f"xsq1_{u}")

    def emit_x16_casts(u):
        for pc in range(NP):
            cdeps = [f"xin{u}"]
            if pc == 0 and u > 0:
                cdeps += [f"ffn1{u-1}_lastmm", f"ffn1{u-1}_zps", f"xsq1_{u-1}",
                          f"ln2{u-1}_q1ps"]
            p.op("s", lambda e, u=u, pc=pc: e.activation(
                out=b16d[:, pc * PW:(pc + 1) * PW], in_=Xb[u % 2][:, pc * PW:(pc + 1) * PW],
                func=AF.Identity),
                deps=tuple(cdeps), tag=f"x16p{u}_{pc}")
        p.alias(f"x16_{u}", f"x16p{u}_{NP-1}")

    for s in range(NS):
        par = s % 2

        if s == 0:
            emit_x16_casts(0)
            emit_xsq1(0)
            gen_tables(0, pcs=[2, 3])
            p.dma("trowg1", lambda e: e.dma_start(
                out=trow[1], in_=trowD[:, SW:2 * SW]),
                deps=("tab0_qi7",), tag="trow_dma1", eng="g")
        if s == 0:
            zdeps = ["vraug", "stats_banks_free"]
            for dc in range(NCH):
                dd = list(zdeps) if dc == 0 else []
                if dc % 2 == 0:
                    dd.append(f"x16p{s}_{dc//2}")
                p.op("t", lambda e, dc=dc: e.matmul(
                    out=psZ, lhsT=VR(dc), rhs=B16(b16d, dc),
                    start=(dc == 0), stop=(dc == NCH - 1)),
                    deps=tuple(dd),
                    tag=f"ffn1{s}_zps" if dc == NCH - 1 else None)
        qb1, qb1dep = next_bank(f"ln1_{s}_srowq_done")
        psQ2a = qb1[0:1, 0:SW]
        qdeps = [f"xsq1_{s}"]
        if qb1dep is not None:
            qdeps.append(qb1dep)
        for dc in range(NCH):
            p.op("t", lambda e, dc=dc, psQ2a=psQ2a: e.matmul(
                out=psQ2a, lhsT=ones16c, rhs=B16(bsq, dc),
                start=(dc == 0), stop=(dc == NCH - 1)),
                 deps=tuple(qdeps) if dc == 0 else (),
                 tag=f"ffn1{s}_q2ps" if dc == NCH - 1 else None)
        if s == 0:
            p.op("v", lambda e: e.tensor_copy(out=z16[0], in_=psZ[0:R, :]),
                 deps=(f"ffn1{s}_zps",), tag=f"z16ready{s}_0")
            p.op("s", lambda e: e.copy(out=SR(0), in_=psZ[R:R + 1, :]),
                 deps=(f"ffn1{s}_zps",), tag=f"ln1_{s}_srow_done")
        p.op("s", lambda e, psQ2a=psQ2a: e.copy(out=SRQ(0), in_=psQ2a),
             deps=(f"ffn1{s}_q2ps",)
                  + ((f"ln1_{s-1}_var",) if s > 0 else ()),
             tag=f"ln1_{s}_srowq_done")

        mnd = [f"ln1_{s}_srow_done"]
        if s > 0:
            mnd.append(f"ffn1{s-1}_lastmm")
        p.op("v", lambda e: e.tensor_scalar(
            out=MN(0), in0=SR(0), scalar1=-1.0 / D, scalar2=None, op0=ALU.mult),
            deps=tuple(mnd), tag=f"ffn1{s}_mneg")
        m2d = [f"ln1_{s}_srow_done"]
        if s > 0:
            m2d.append(f"ln1_{s-1}_var")
        p.op("s", lambda e: e.activation(out=SVT(0), in_=SR(0), func=AF.Square,
                                         scale=1.0 / D),
             deps=tuple(m2d), tag=f"ln1_{s}_m2")
        p.op("v", lambda e: e.scalar_tensor_tensor(
            out=svar, in0=SRQ(0), scalar=1.0 / D, in1=SVT(0),
            op0=ALU.mult, op1=ALU.subtract),
            deps=(f"ln1_{s}_srowq_done", f"ln1_{s}_m2")
                 + ((f"ln3_{s-1}_ln",) if s > 0 else ()), tag=f"ln1_{s}_var")
        lndeps = [f"ln1_{s}_var"]
        if s > 0:
            lndeps += [f"ln3_{s-1}_psR", f"ln3_{s-1}_sd", f"ln3_{s-1}_rr"]
        p.op("s", lambda e: e.activation(out=lnrow, in_=svar, func=AF.Ln,
                                         bias=epscol[0:1, :]),
             deps=tuple(lndeps), tag=f"ln1_{s}_ln")
        p.op("s", lambda e: e.activation(out=rrow, in_=lnrow, func=AF.Exp,
                                         scale=mhcol[0:1, :]),
             deps=(f"ln1_{s}_ln",) + ((f"ln3_{s-1}_psR",) if s > 0 else ()),
             tag=f"ln1_{s}_rr")
        sdd = [f"ln1_{s}_ln"]
        if s > 0:
            sdd.append(f"ffn2{s-1}_bK1")
        p.op("s", lambda e: e.activation(out=sd16, in_=lnrow, func=AF.Exp,
                                         scale=phcol[0:1, :]),
             deps=tuple(sdd), tag=f"ln1_{s}_sd")
        def emit_psR1():
            rdeps = [f"ln1_{s}_rr", "stats_banks_free"]
            if s > 0:
                rdeps.append(f"ln3_{s-1}_R")
            p.op("t", lambda e: e.matmul(out=psR, lhsT=halff32r, rhs=rrow,
                                         start=True, stop=True),
                 deps=tuple(rdeps), tag=f"ln1_{s}_psR")
            rcdeps = [f"ln1_{s}_psR"]
            if s > 0:
                rcdeps.append(f"ffn2{s-1}_Rused")
            p.op("s", lambda e: e.copy(out=Rrep[0], in_=psR),
                 deps=tuple(rcdeps), tag=f"ln1_{s}_R")

        def proj_ffn(name, zbuf_idx, mneg_ln, rrep_idx, rsrc_tag, xdep_fn,
                     mid_emit=None, xsrc=None):
            banks = {}

            def mains(pc):
                bank, bdep = next_bank(f"{name}_t{pc}")
                banks[pc] = bank
                mmdeps = ["w16"]
                if bdep is not None:
                    mmdeps.append(bdep)
                for h in range(2):
                    oc = 2 * pc + h
                    dst = bank[:, h * SW:(h + 1) * SW]
                    for dc in range(NCH):
                        dd = list(mmdeps) if (h == 0 and dc == 0) else []
                        if dc % 2 == 0:
                            dd.append(xdep_fn(dc // 2))
                        p.op("t", lambda e, dc=dc, oc=oc, dst=dst, xsrc=xsrc: e.matmul(
                            out=dst, lhsT=Wsl(w16, dc, oc), rhs=B16(xsrc, dc),
                            start=(dc == 0), stop=False),
                            deps=tuple(dd))

            def tail(pc):
                bank = banks[pc]
                for h in range(2):
                    oc = 2 * pc + h
                    dst = bank[:, h * SW:(h + 1) * SW]
                    p.op("t", lambda e, oc=oc, dst=dst, zbuf_idx=zbuf_idx: e.matmul(
                        out=dst, lhsT=cTb[:, oc * 128:(oc + 1) * 128], rhs=z16[zbuf_idx],
                        start=False, stop=False),
                        deps=(f"{name}_zdep", "smallg_dma") if (pc == 0 and h == 0) else (),
                        tag=f"{name}_z2done" if oc == NCH - 1 else None)
                    p.op("t", lambda e, oc=oc, dst=dst, mneg_ln=mneg_ln: e.matmul(
                        out=dst, lhsT=lhs_stot(oc), rhs=MN(mneg_ln),
                        start=False, stop=False),
                        deps=(f"{name}_mnegdep",) if (pc == 0 and h == 0) else ())
                    p.op("t", lambda e, oc=oc, dst=dst: e.matmul(
                        out=dst, lhsT=lhs_b(oc), rhs=sd16,
                        start=False, stop=True),
                        deps=(f"{name}_sddep", "smallg_dma") if (pc == 0 and h == 0) else (),
                        tag=f"{name}_mm{pc}" if h == 1 else None)

            evq = []

            def evac(pc):
                bank = banks[pc]
                edeps = [f"{name}_mm{pc}", rsrc_tag]
                if pc > 0:
                    edeps.append(f"{name}_ev{pc-1}")
                else:
                    edeps.append(f"{name}_scrdep")
                p.op("v", lambda e, bank=bank, rrep_idx=rrep_idx: e.tensor_tensor(
                    out=scrE, in0=bank, in1=rep2(Rrep[rrep_idx]), op=ALU.mult),
                    deps=tuple(edeps), tag=f"{name}_t{pc}")
                p.op("v", lambda e, par=par, pc=pc: e.tensor_tensor(
                    out=XP(par, pc), in0=scrE, in1=XP(par, pc), op=ALU.add),
                    deps=(f"{name}_t{pc}",), tag=f"{name}_ev{pc}")
                drip(2)

            mains(0)
            mains(1)
            if mid_emit is not None:
                mid_emit()
            tail(0)
            evac(0)
            mains(2)
            tail(1)
            evac(1)
            mains(3)
            tail(2)
            evac(2)
            tail(3)
            evac(3)
            p.alias(f"{name}_bK1", f"{name}_mm{NP-1}")
            p.alias(f"{name}_Rused", f"{name}_t{NP-1}")
            p.alias(f"{name}_lastmm", f"{name}_mm{NP-1}")

        if s + 1 < NS:
            if s + 1 >= 2:
                p.dma(f"trowg{s + 1}", lambda e, s=s: e.dma_start(
                    out=trow[(s + 1) % 2], in_=trowD[:, (s + 1) * SW:(s + 2) * SW]),
                    deps=(f"tab{s}_qi{NCH-1}",), tag=f"trow_dma{s+1}", eng="g")
            if s >= 1:
                gen_tables(s + 1, defer=True)

        nm = f"ffn1{s}"
        if s > 0:
            p.alias(f"{nm}_bankdep0", f"ffn2{s-1}_ev{NP-2}")
            p.alias(f"{nm}_bankdep1", f"ffn2{s-1}_ev{NP-1}")
            p.alias(f"{nm}_scrdep", f"ffn2{s-1}_ev{NP-1}")
        else:
            p.alias(f"{nm}_bankdep0", "stats_banks_free")
            p.alias(f"{nm}_bankdep1", "stats_banks_free")
            p.alias(f"{nm}_scrdep", "memsets")
        p.alias(f"{nm}_zdep", f"z16ready{s}_0")
        p.alias(f"{nm}_mnegdep", f"{nm}_mneg")
        p.alias(f"{nm}_sddep", f"ln1_{s}_sd")
        proj_ffn(nm, 0, 0, 0, f"ln1_{s}_R", lambda pc2, s=s: f"x16p{s}_{pc2}",
                 mid_emit=emit_psR1, xsrc=b16d)

        for pc in range(NP):
            c1deps = [f"ffn1{s}_ev{pc}"]
            if pc == 0 and s > 0:
                c1deps.append(f"wo{s-1}_lastmm")
            p.op("s", lambda e, par=par, pc=pc: e.activation(
                out=b16b[:, pc * PW:(pc + 1) * PW], in_=XP(par, pc),
                func=AF.Identity),
                deps=tuple(c1deps), tag=f"x116p{s}_{pc}")
        p.alias(f"x116_{s}", f"x116p{s}_{NP-1}")

        p.op("s", lambda e: e.activation(out=bsq, in_=b16b, func=AF.Square),
             deps=(f"x116_{s}", f"ffn1{s}_q2ps"), tag=f"xsq2_{s}")
        qb2, qb2dep = next_bank(f"ln2_{s}_srowq_done")
        psQ1b = qb2[0:1, 0:SW]
        psQ2b = qb2[0:1, SW:PW]
        q1deps = [f"x116_{s}"]
        if qb2dep is not None:
            q1deps.append(qb2dep)
        for dc in range(NCH):
            p.op("t", lambda e, dc=dc, psQ1b=psQ1b: e.matmul(
                out=psQ1b, lhsT=ones16c, rhs=B16(b16b, dc),
                start=(dc == 0), stop=(dc == NCH - 1)),
                 deps=tuple(q1deps) if dc == 0 else (),
                 tag=f"ln2{s}_q1ps" if dc == NCH - 1 else None)
        def emit_q2_ln2():
            for dc in range(NCH):
                p.op("t", lambda e, dc=dc, psQ2b=psQ2b: e.matmul(
                    out=psQ2b, lhsT=ones16c, rhs=B16(bsq, dc),
                    start=(dc == 0), stop=(dc == NCH - 1)),
                    deps=(f"xsq2_{s}",) if dc == 0 else (),
                    tag=f"ln2{s}_q2ps" if dc == NCH - 1 else None)
        p.op("s", lambda e, psQ1b=psQ1b: e.copy(out=SR(1), in_=psQ1b),
             deps=(f"ln2{s}_q1ps",)
                  + ((f"v{s-1}_mm{NP-1}", f"ln2_{s-1}_var", f"ln2_{s-1}_m2") if s > 0 else ()),
             tag=f"ln2_{s}_srow_done")
        p.op("s", lambda e, psQ2b=psQ2b: e.copy(out=SRQ(1), in_=psQ2b),
             deps=(f"ln2{s}_q2ps",) + ((f"ln2_{s-1}_var",) if s > 0 else ()),
             tag=f"ln2_{s}_srowq_done")
        mnd2 = [f"ln2_{s}_srow_done"]
        if s > 0:
            mnd2.append(f"v{s-1}_mm{NP-1}")
        p.op("v", lambda e: e.tensor_scalar(
            out=MN(1), in0=SR(1), scalar1=-1.0 / D, scalar2=None, op0=ALU.mult),
            deps=tuple(mnd2), tag=f"v{s}_mneg")
        p.op("s", lambda e: e.activation(out=SVT(1), in_=SR(1), func=AF.Square,
                                         scale=1.0 / D),
             deps=(f"ln2_{s}_srow_done", f"ln1_{s}_rr", f"ln1_{s}_sd")
                  + ((f"ln2_{s-1}_var",) if s > 0 else ()), tag=f"ln2_{s}_m2")
        p.op("v", lambda e: e.scalar_tensor_tensor(
            out=svar, in0=SRQ(1), scalar=1.0 / D, in1=SVT(1),
            op0=ALU.mult, op1=ALU.subtract),
            deps=(f"ln2_{s}_srowq_done", f"ln2_{s}_m2", f"ln1_{s}_ln"),
            tag=f"ln2_{s}_var")
        p.op("s", lambda e: e.activation(out=lnrow, in_=svar, func=AF.Ln,
                                         bias=epscol[0:1, :]),
             deps=(f"ln2_{s}_var", f"ln1_{s}_psR", f"ln1_{s}_sd"), tag=f"ln2_{s}_ln")
        p.op("s", lambda e: e.activation(out=rrow, in_=lnrow, func=AF.Exp,
                                         scale=mhcol[0:1, :]),
             deps=(f"ln2_{s}_ln",), tag=f"ln2_{s}_rr")
        def emit_psR2():
            p.op("t", lambda e: e.matmul(out=psR, lhsT=onesf32r, rhs=rrow,
                                         start=True, stop=True),
                 deps=(f"ln2_{s}_rr", f"ln1_{s}_R"), tag=f"ln2_{s}_psR")
        rc2 = [f"ln2_{s}_psR"]
        if s > 0:
            rc2.append(f"v{s-1}_Rused")
        p.op("s", lambda e: e.copy(out=Rrep[1], in_=psR),
             deps=tuple(rc2), tag=f"ln2_{s}_R")
        if s + 1 < NS:
            emit_x16_casts(s + 1)


        if s == 0:
            drip_all()
            gen_tables(1, defer=True)

        def emit_recombine(pc):
            t1deps = [f"scanc{s}_{2*pc}", f"scanc{s}_{2*pc+1}", f"ln2{s}_q2ps"]
            if pc > 0:
                t1deps.append(f"ret{s}_{pc-1}")
            p.op("v", lambda e, s=s, pc=pc: e.tensor_tensor(
                out=bsq[:, pc * PW:(pc + 1) * PW], in0=AcP(pc), in1=TCP(s, pc),
                op=ALU.mult),
                deps=tuple(t1deps), tag=f"t1_{s}_{pc}")
            t2deps = [f"scans{s}_{2*pc}", f"scans{s}_{2*pc+1}",
                      f"car{s}_s{2*pc}", f"car{s}_s{2*pc+1}"]
            p.op("v", lambda e, s=s, pc=pc: e.tensor_tensor(
                out=AsP(pc), in0=AsP(pc), in1=TSP(s, pc), op=ALU.mult),
                deps=tuple(t2deps), tag=f"t2_{s}_{pc}")
            p.op("v", lambda e, s=s, pc=pc: e.tensor_tensor(
                out=bsq[:, pc * PW:(pc + 1) * PW],
                in0=bsq[:, pc * PW:(pc + 1) * PW], in1=AsP(pc), op=ALU.add),
                deps=(f"t1_{s}_{pc}", f"t2_{s}_{pc}"), tag=f"ret{s}_{pc}")
            drip(2)

        if s == 0:
            vpz_free = [f"z16ready{0}_0", f"ln1_{0}_srow_done"]
        else:
            vpz_free = [f"z16ready{s-1}_1", f"ln3_{s-1}_srow_done"]
        for pc in range(NP):
            if pc == 0:
                emit_q2_ln2()
            if pc == 1:
                emit_psR2()
            if pc < NP - 1:
                bank, bdep = next_bank(f"v{s}_vv{pc}")
                halves = [bank[:, 0:SW], bank[:, SW:PW]]
                mmdeps = ["wv16"]
                if bdep is not None:
                    mmdeps.append(bdep)
            else:
                halves = [psZB, psR]
                mmdeps = ["wv16"] + vpz_free
            for h in range(2):
                oc = 2 * pc + h
                dst = halves[h]
                for dc in range(NCH):
                    dd = list(mmdeps) if (h == 0 and dc == 0) else []
                    if pc == NP - 1 and h == 1 and dc == 0:
                        dd.append(f"ln2_{s}_R")
                    if dc % 2 == 0:
                        dd.append(f"x116p{s}_{dc//2}")
                    p.op("t", lambda e, dc=dc, oc=oc, dst=dst: e.matmul(
                        out=dst, lhsT=Wsl(wv16, dc, oc), rhs=B16(b16b, dc),
                        start=(dc == 0), stop=False),
                        deps=tuple(dd))
                p.op("t", lambda e, oc=oc, dst=dst: e.matmul(
                    out=dst, lhsT=lhs_swv(oc), rhs=MN(1),
                    start=False, stop=True),
                    deps=(f"v{s}_mneg",) if (pc == 0 and h == 0) else (),
                    tag=f"v{s}_mm{pc}" if h == 1 else None)
            mdeps = [f"v{s}_mm{pc}", f"ln2_{s}_R", f"tab{s}_done"]
            if s > 0 and pc < 2:
                mdeps.append(f"recomb{s-1}_done")
            elif s > 0 and pc == 2:
                mdeps.append(f"scan{s-1}_done")
            if pc < NP - 1:
                p.op("v", lambda e, s=s, pc=pc, bank=bank: e.tensor_tensor(
                    out=AcP(pc), in0=bank, in1=rep2(Rrep[1]), op=ALU.mult),
                    deps=tuple(mdeps), tag=f"v{s}_vv{pc}")
            else:
                p.op("v", lambda e, s=s, pc=pc: e.tensor_tensor(
                    out=AcP(pc)[:, 0:SW], in0=psZB, in1=Rrep[1], op=ALU.mult),
                    deps=tuple(mdeps), tag=f"v{s}_vv3a")
                p.op("v", lambda e, s=s, pc=pc: e.tensor_tensor(
                    out=AcP(pc)[:, SW:PW], in0=psR, in1=Rrep[1], op=ALU.mult),
                    deps=(f"v{s}_vv3a",), tag=f"v{s}_vv{pc}")
            p.op("v", lambda e, s=s, pc=pc: e.tensor_tensor(
                out=AsP(pc), in0=AcP(pc), in1=TSP(s, pc), op=ALU.mult),
                deps=(f"v{s}_vv{pc}",), tag=f"v{s}_as{pc}")
            p.op("v", lambda e, s=s, pc=pc: e.tensor_tensor(
                out=AcP(pc), in0=AcP(pc), in1=TCP(s, pc), op=ALU.mult),
                deps=(f"v{s}_as{pc}",), tag=f"v{s}_ac{pc}")
            drip(2)
            if pc > 0:
                emit_recombine(pc - 1)
            for oc in (2 * pc, 2 * pc + 1):
                if s == 0:
                    p.op("v", lambda e, oc=oc: e.tensor_tensor_scan(
                        out=Ac1(oc), data0=Ac1(oc), data1=Ac1(oc), initial=0.0,
                        op0=ALU.add, op1=ALU.bypass),
                        deps=(f"v{s}_ac{pc}",), tag=f"scanc{s}_{oc}")
                    p.op("v", lambda e, oc=oc: e.tensor_tensor_scan(
                        out=As1(oc), data0=As1(oc), data1=As1(oc), initial=0.0,
                        op0=ALU.add, op1=ALU.bypass),
                        deps=(f"v{s}_as{pc}",), tag=f"scans{s}_{oc}")
                else:
                    p.op("v", lambda e, oc=oc: e.tensor_tensor_scan(
                        out=Ac1(oc), data0=Ac1(oc), data1=Ac1(oc),
                        initial=ccar[:, oc:oc + 1], op0=ALU.add, op1=ALU.bypass),
                        deps=(f"v{s}_ac{pc}", f"car{s-1}_c{oc}"), tag=f"scanc{s}_{oc}")
                    p.op("v", lambda e, oc=oc: e.tensor_tensor_scan(
                        out=As1(oc), data0=As1(oc), data1=As1(oc),
                        initial=ccar[:, NCH + oc:NCH + oc + 1], op0=ALU.add,
                        op1=ALU.bypass),
                        deps=(f"v{s}_as{pc}", f"car{s-1}_s{oc}"), tag=f"scans{s}_{oc}")
                p.op("s", lambda e, oc=oc: e.copy(
                    out=ccar[:, oc:oc + 1], in_=Ac1(oc)[:, SW - 1:SW]),
                    deps=(f"scanc{s}_{oc}",), tag=f"car{s}_c{oc}")
                p.op("s", lambda e, oc=oc: e.copy(
                    out=ccar[:, NCH + oc:NCH + oc + 1], in_=As1(oc)[:, SW - 1:SW]),
                    deps=(f"scans{s}_{oc}",), tag=f"car{s}_s{oc}")
                drip(2)
        p.alias(f"v{s}_Rused", f"v{s}_vv{NP-1}")
        p.alias(f"scan{s}_done", f"car{s}_s{NCH-1}")
        emit_recombine(NP - 1)
        p.alias(f"recomb{s}_done", f"ret{s}_{NP-1}")

        if s + 1 < NS:
            u = s + 1
            zdeps = [f"ffn1{s}_z2done", f"v{s}_vv3a"]
            for dc in range(NCH):
                dd = list(zdeps) if dc == 0 else []
                if dc % 2 == 0:
                    dd.append(f"x16p{u}_{dc//2}")
                p.op("t", lambda e, dc=dc: e.matmul(
                    out=psZ, lhsT=VR(dc), rhs=B16(b16d, dc),
                    start=(dc == 0), stop=(dc == NCH - 1)),
                    deps=tuple(dd),
                    tag=f"ffn1{u}_zps" if dc == NCH - 1 else None)
            p.op("s", lambda e: e.activation(out=z16[0], in_=psZ[0:R, :],
                                             func=AF.Identity),
                 deps=(f"ffn1{u}_zps",), tag=f"z16ready{u}_0")
            p.op("s", lambda e: e.copy(out=SR(0), in_=psZ[R:R + 1, :]),
                 deps=(f"ffn1{u}_zps", f"ln1_{s}_var", f"ln1_{s}_m2"),
                 tag=f"ln1_{u}_srow_done")

        for pc in range(NP - 1):
            bank, bdep = next_bank(f"wo{s}_ev{pc}")
            mmdeps = [f"wo16"]
            if bdep is not None:
                mmdeps.append(bdep)
            for h in range(2):
                oc = 2 * pc + h
                dst = bank[:, h * SW:(h + 1) * SW]
                for dc in range(NCH):
                    dd = list(mmdeps) if (h == 0 and dc == 0) else []
                    if dc % 2 == 0:
                        dd.append(f"ret{s}_{dc//2}")
                    p.op("t", lambda e, dc=dc, oc=oc, dst=dst: e.matmul(
                        out=dst, lhsT=Wsl(wo16, dc, oc), rhs=B16(bsq, dc),
                        start=(dc == 0), stop=(dc == NCH - 1)),
                        deps=tuple(dd),
                        tag=f"wo{s}_mm{pc}" if (h == 1 and dc == NCH - 1) else None)
            evdeps = [f"wo{s}_mm{pc}"]
            if pc > 0:
                evdeps.append(f"wo{s}_ev{pc-1}")
            else:
                evdeps.append(f"ffn1{s}_ev{NP-1}")
            p.op("v", lambda e, par=par, pc=pc, bank=bank: e.tensor_tensor(
                out=XP(par, pc), in0=bank, in1=XP(par, pc), op=ALU.add),
                deps=tuple(evdeps), tag=f"wo{s}_ev{pc}")
            drip(2)
        if s + 1 < NS:
            pz_free = [f"z16ready{s+1}_0", f"ln1_{s+1}_srow_done"]
        else:
            pz_free = [f"v{s}_vv3a"]
        for h, (dst, extra) in enumerate([(psZB, pz_free),
                                          (psR, [f"v{s}_vv{NP-1}"])]):
            oc = 2 * (NP - 1) + h
            for dc in range(NCH):
                dd = (["wo16"] + extra) if dc == 0 else []
                if dc % 2 == 0:
                    dd.append(f"ret{s}_{dc//2}")
                p.op("t", lambda e, dc=dc, oc=oc, dst=dst: e.matmul(
                    out=dst, lhsT=Wsl(wo16, dc, oc), rhs=B16(bsq, dc),
                    start=(dc == 0), stop=(dc == NCH - 1)),
                    deps=tuple(dd),
                    tag=f"wo{s}_mm3{'ab'[h]}" if dc == NCH - 1 else None)
        p.alias(f"wo{s}_mm{NP-1}", f"wo{s}_mm3b")
        p.op("v", lambda e, par=par: e.tensor_tensor(
            out=XP(par, NP - 1)[:, 0:SW], in0=psZB, in1=XP(par, NP - 1)[:, 0:SW],
            op=ALU.add),
            deps=(f"wo{s}_mm3a", f"wo{s}_ev{NP-2}"), tag=f"wo{s}_ev3a")
        p.op("v", lambda e, par=par: e.tensor_tensor(
            out=XP(par, NP - 1)[:, SW:PW], in0=psR, in1=XP(par, NP - 1)[:, SW:PW],
            op=ALU.add),
            deps=(f"wo{s}_mm3b", f"wo{s}_ev3a"), tag=f"wo{s}_ev{NP-1}")
        drip(2)
        p.alias(f"wo{s}_lastmm", f"wo{s}_mm{NP-1}")

        for pc in range(NP):
            c2deps = [f"wo{s}_ev{pc}"]
            if pc == 0:
                c2deps += [f"ffn1{s}_lastmm", f"ffn1{s}_zps", f"xsq1_{s}"]
            p.op("s", lambda e, par=par, pc=pc: e.activation(
                out=b16a[:, pc * PW:(pc + 1) * PW], in_=XP(par, pc),
                func=AF.Identity),
                deps=tuple(c2deps),
                tag=f"x216p{s}_{pc}" if True else None)
        p.alias(f"x216_{s}", f"x216p{s}_{NP-1}")
        p.op("s", lambda e: e.activation(out=bsq, in_=b16a, func=AF.Square),
             deps=(f"x216_{s}", f"ln2{s}_q2ps"), tag=f"xsq3_{s}")
        z2deps = [f"z16ready{s}_0", f"ln1_{s}_srow_done", f"wo{s}_ev3a"]
        if s + 1 < NS:
            z2deps += [f"z16ready{s+1}_0", f"ln1_{s+1}_srow_done"]
        if s > 0:
            z2deps.append(f"ffn2{s-1}_z2done")
        for dc in range(NCH):
            dd = list(z2deps) if dc == 0 else []
            if dc % 2 == 0:
                dd.append(f"x216p{s}_{dc//2}")
            p.op("t", lambda e, dc=dc: e.matmul(out=psZ, lhsT=VR(dc), rhs=B16(b16a, dc),
                                                start=(dc == 0), stop=(dc == NCH - 1)),
                 deps=tuple(dd),
                 tag=f"ffn2{s}_zps" if dc == NCH - 1 else None)
        qb3, qb3dep = next_bank(f"ln3_{s}_srowq_done")
        psQ2c = qb3[0:1, 0:SW]
        q3deps = [f"xsq3_{s}"]
        if qb3dep is not None:
            q3deps.append(qb3dep)
        for dc in range(NCH):
            p.op("t", lambda e, dc=dc, psQ2c=psQ2c: e.matmul(
                out=psQ2c, lhsT=ones16c, rhs=B16(bsq, dc),
                start=(dc == 0), stop=(dc == NCH - 1)),
                 deps=tuple(q3deps) if dc == 0 else (),
                 tag=f"ffn2{s}_q2ps" if dc == NCH - 1 else None)
        zev2 = [f"ffn2{s}_zps"]
        if s > 0:
            zev2.append(f"ffn2{s-1}_z2done")
        p.op("v", lambda e: e.tensor_copy(out=z16[1], in_=psZ[0:R, :]),
             deps=tuple(zev2), tag=f"z16ready{s}_1")
        p.op("s", lambda e: e.copy(out=SR(2), in_=psZ[R:R + 1, :]),
             deps=(f"ffn2{s}_zps", f"ln2_{s}_var")
                  + ((f"ffn2{s-1}_lastmm", f"ln3_{s-1}_var", f"ln3_{s-1}_m2") if s > 0 else ()),
             tag=f"ln3_{s}_srow_done")
        p.op("s", lambda e, psQ2c=psQ2c: e.copy(out=SRQ(2), in_=psQ2c),
             deps=(f"ffn2{s}_q2ps",) + ((f"ln3_{s-1}_var",) if s > 0 else ()),
             tag=f"ln3_{s}_srowq_done")
        mnd3 = [f"ln3_{s}_srow_done"]
        if s > 0:
            mnd3.append(f"ffn2{s-1}_lastmm")
        p.op("v", lambda e: e.tensor_scalar(
            out=MN(2), in0=SR(2), scalar1=-1.0 / D, scalar2=None, op0=ALU.mult),
            deps=tuple(mnd3), tag=f"ffn2{s}_mneg")
        p.op("s", lambda e: e.activation(out=SVT(2), in_=SR(2), func=AF.Square,
                                         scale=1.0 / D),
             deps=(f"ln3_{s}_srow_done", f"ln2_{s}_rr")
                  + ((f"ln3_{s-1}_var",) if s > 0 else ()), tag=f"ln3_{s}_m2")
        p.op("v", lambda e: e.scalar_tensor_tensor(
            out=svar, in0=SRQ(2), scalar=1.0 / D, in1=SVT(2),
            op0=ALU.mult, op1=ALU.subtract),
            deps=(f"ln3_{s}_srowq_done", f"ln3_{s}_m2", f"ln2_{s}_ln"),
            tag=f"ln3_{s}_var")
        p.op("s", lambda e: e.activation(out=lnrow, in_=svar, func=AF.Ln,
                                         bias=epscol[0:1, :]),
             deps=(f"ln3_{s}_var", f"ln2_{s}_psR", f"ln1_{s}_sd"), tag=f"ln3_{s}_ln")
        p.op("s", lambda e: e.activation(out=rrow, in_=lnrow, func=AF.Exp,
                                         scale=mhcol[0:1, :]),
             deps=(f"ln3_{s}_ln",), tag=f"ln3_{s}_rr")
        p.op("s", lambda e: e.activation(out=sd16, in_=lnrow, func=AF.Exp,
                                         scale=phcol[0:1, :]),
             deps=(f"ln3_{s}_ln", f"ffn1{s}_bK1"), tag=f"ln3_{s}_sd")
        def emit_psR3():
            p.op("t", lambda e: e.matmul(out=psR, lhsT=halff32r, rhs=rrow,
                                         start=True, stop=True),
                 deps=(f"ln3_{s}_rr", f"ln2_{s}_R", f"wo{s}_ev{NP-1}"),
                 tag=f"ln3_{s}_psR")
        p.op("s", lambda e: e.copy(out=Rrep[0], in_=psR),
             deps=(f"ln3_{s}_psR", f"ffn1{s}_Rused"), tag=f"ln3_{s}_R")
        if s + 1 < NS:
            emit_xsq1(s + 1)

        nm = f"ffn2{s}"
        p.alias(f"{nm}_bankdep0", f"wo{s}_ev{NP-2}")
        p.alias(f"{nm}_bankdep1", f"wo{s}_ev{NP-1}")
        p.alias(f"{nm}_scrdep", f"ffn1{s}_ev{NP-1}")
        p.alias(f"{nm}_zdep", f"z16ready{s}_1")
        p.alias(f"{nm}_mnegdep", f"{nm}_mneg")
        p.alias(f"{nm}_sddep", f"ln3_{s}_sd")
        proj_ffn(nm, 1, 2, 0, f"ln3_{s}_R", lambda pc2, s=s: f"x216p{s}_{pc2}",
                 mid_emit=emit_psR3, xsrc=b16a)

        drip_all()
        for oc in range(NCH):
            pc = oc // 2
            p.dma(f"o{par}", lambda e, s=s, par=par, oc=oc: e.dma_start(
                out=out[oc * 128:(oc + 1) * 128, s * SW:(s + 1) * SW],
                in_=Xb[par][:, oc * SW:(oc + 1) * SW]),
                deps=(f"ffn2{s}_ev{pc}",) if oc % 2 == 0 else (),
                tag=f"outdma{s}" if oc == NCH - 1 else None)
        if s + 2 < NS:
            load_x(s + 2, (f"outdma{s}",))

    s0, v0 = p.dma_total("o0")
    s1, v1 = p.dma_total("o1")

    def final_wait(e):
        e.wait_ge(s0, v0)
        e.wait_ge(s1, v1)
        return None
    p.op("d", final_wait)

    assert p.check(), "plan deadlock"
    with nc.Block() as block:
        p.emit(block)
    return nc


_NC_CACHE = None


def kernel(L0_attn_x, L0_attn_Wk, L0_attn_Wv, L0_attn_Wo, L0_attn_freqs,
           L0_ffn_W, L0_ffn_b, L0_ffn_V_r, L0_ffn_C):
    global _NC_CACHE
    x = np.asarray(L0_attn_x, dtype=np.float32)
    B = x.shape[0]
    wTh = np.ascontiguousarray(np.asarray(L0_ffn_W, np.float32).T)
    wvTh = np.ascontiguousarray(np.asarray(L0_attn_Wv, np.float32).T)
    woTh = np.ascontiguousarray(np.asarray(L0_attn_Wo, np.float32).T)
    vrh = np.ascontiguousarray(np.asarray(L0_ffn_V_r, np.float32))
    cTh = np.ascontiguousarray(np.asarray(L0_ffn_C, np.float32).T)
    browh = np.ascontiguousarray(np.asarray(L0_ffn_b, np.float32).reshape(1, D))
    fcolh = np.ascontiguousarray(np.asarray(L0_attn_freqs, np.float32).reshape(NCH, 128).T)
    trowh = np.ascontiguousarray(np.tile(np.arange(T, dtype=np.float32), (128, 1)))

    if _NC_CACHE is None:
        _NC_CACHE = build_program()
    nc = _NC_CACHE

    in_maps = []
    for c in range(B):
        in_maps.append({
            "x": np.ascontiguousarray(x[c].T),
            "wT": wTh, "wvT": wvTh, "woT": woTh, "vr": vrh, "cT": cTh,
            "brow": browh, "fcol": fcolh, "trow": trowh,
        })
    res = run_bass_kernel_spmd(nc, in_maps, core_ids=list(range(NC)))
    outs = [np.ascontiguousarray(np.asarray(r["out"]).T) for r in res.results]
    return np.stack(outs, axis=0).astype(np.float32)
